# version 29
# baseline (speedup 1.0000x reference)
import numpy as np
import ml_dtypes
import concourse.bass as bass
import concourse.mybir as mybir
from concourse.bass_utils import run_bass_kernel_spmd
from contextlib import ExitStack

F32 = mybir.dt.float32
BF16 = mybir.dt.bfloat16
I32 = mybir.dt.int32
U8 = mybir.dt.uint8
AF = mybir.ActivationFunctionType
ALU = mybir.AluOpType
AX = mybir.AxisListType
f32 = np.float32

N_CORES = 8
NL = 8
T = NL * 128
NF = 16
NH = 44
HWIN = 4
NWIN = 11
W3G = 4
D = 2048
H = 5632
K_HORNER_F = 12
K_HORNER_S = 6
RG = [list(range(N_CORES))]


class Prog:
    ENGINES = ("t", "v", "s", "y", "g")
    DMA_ENGINES = ("y", "g")

    QPAR = {"y": ("ya", "yb"), "w": ("za", "zb")}

    def __init__(self):
        self.items = []
        self.cnt = {k: 0 for k in self.ENGINES}
        self.cnt["w"] = 0
        self.cnt["z"] = 0
        for q, (a, b) in self.QPAR.items():
            self.cnt[a] = 0
            self.cnt[b] = 0
        self.res = {}
        self.excl_ids = set()
        self.dma_base = {k: 0 for k in ("y", "g", "ya", "yb", "za", "zb")}
        self.par = {"y": "ya", "w": "za"}
        self.dma_pend = {k: {"w": [], "r": []} for k in ("y", "g", "w")}

    def mark_excl(self, h):
        self.excl_ids.add(id(h))

    def _res(self, h):
        key = id(h)
        if key not in self.res:
            self.res[key] = {"w": [], "r": [], "excl": key in self.excl_ids,
                             "pend_q": None, "pend_w": False}
        return self.res[key]

    def flush(self, q):
        sq = self.par[q] if q in self.QPAR else q
        if self.cnt[sq] == self.dma_base[sq]:
            return
        self.dma_base[sq] = self.cnt[sq]
        tok = (sq, self.dma_base[sq])
        if q in self.QPAR:
            a, b = self.QPAR[q]
            self.par[q] = b if self.par[q] == a else a
        for h in self.dma_pend[q]["w"]:
            r = self._res(h)
            r["w"].append(tok)
            r["r"] = []
            r["pend_q"] = None
            r["pend_w"] = False
        for h in self.dma_pend[q]["r"]:
            r = self._res(h)
            r["r"].append(tok)
            r["pend_q"] = None
        self.dma_pend[q] = {"w": [], "r": []}
        return tok

    def _settle(self, h):
        r = self._res(h)
        if r["pend_q"] is not None:
            self.flush(r["pend_q"])

    def note_write(self, h, tok):
        self._settle(h)
        r = self._res(h)
        r["w"] = [tok]
        r["r"] = []

    def op(self, eng, fn, reads=(), writes=(), after=(), inc=None,
           register=True, sem_key=None, quiet=False):
        is_dma = eng in self.DMA_ENGINES or eng == "w"
        if inc is None:
            inc = 16 if is_dma else 1
        if quiet:
            inc = 0
        waits = list(t for t in after if t is not None)
        for h in reads:
            r = self._res(h)
            if r["pend_w"] or (r["excl"] and r["pend_q"] is not None):
                self.flush(r["pend_q"])
        for h in writes:
            r = self._res(h)
            if r["pend_q"] is not None:
                if is_dma and r["pend_q"] == eng and r["pend_w"]:
                    continue
                self.flush(r["pend_q"])
        skey = sem_key or (self.par[eng] if eng in self.QPAR else eng)
        if is_dma and self.dma_base[skey] > 0:
            waits.append((skey, self.dma_base[skey]))
        assert not (is_dma and sem_key), "sem_key dma uses own discipline"
        for h in reads:
            r = self._res(h)
            waits.extend(r["w"])
            if r["excl"]:
                waits.extend(r["r"])
        for h in writes:
            r = self._res(h)
            waits.extend(r["w"])
            waits.extend(r["r"])
        tok = (skey, self.cnt[skey] + inc)
        if register:
            for h in reads:
                r = self._res(h)
                if is_dma:
                    self.dma_pend[eng]["r"].append(h)
                    r["pend_q"] = eng
                elif r["excl"]:
                    r["w"] = [tok]
                    r["r"] = []
                else:
                    r["r"].append(tok)
            for h in writes:
                r = self._res(h)
                if is_dma:
                    self.dma_pend[eng]["w"].append(h)
                    r["pend_q"] = eng
                    r["pend_w"] = True
                else:
                    r["w"] = [tok]
                    r["r"] = []
        best = {}
        for sk, sv in waits:
            best[sk] = max(best.get(sk, 0), sv)
        if skey in best:
            assert best[skey] <= self.cnt[skey], (skey, best[skey],
                                                  self.cnt[skey])
        self.items.append((eng, sorted(best.items()), fn, inc, skey))
        self.cnt[skey] += inc
        if inc == 0 and not quiet:
            return None
        return tok

    def emit(self, block, sems):
        names = {"t": "tensor", "v": "vector", "s": "scalar",
                 "y": "sync", "g": "gpsimd", "w": "scalar"}
        blockkeys = {}
        for it in self.items:
            blockkeys.setdefault(names[it[0]], []).append(it)
        for bname, items in blockkeys.items():
            def stream(e, items=items):
                seen = {k: 0 for k in sems}
                for _, waits, fn, inc, skey in items:
                    for sk, sv in waits:
                        if sv > seen[sk]:
                            e.wait_ge(sems[sk], sv)
                            seen[sk] = sv
                    inst = fn(e)
                    if inst is not None and inc > 0:
                        inst.then_inc(sems[skey], inc)

            getattr(block, bname)(stream)


def np_consts():
    i = np.arange(128)
    p, fr = i[:, None], i[None, :]
    e127 = np.zeros((128, 1), np.float32)
    e127[127, 0] = 1.0
    return {
        "c_masklo": (fr < p).astype(np.float32),
        "c_maskup": (fr > p).astype(np.float32),
        "c_maskup0": (fr >= p).astype(np.float32),
        "c_eye": np.eye(128, dtype=np.float32),
        "c_ones": np.ones((128, 128), dtype=np.float32),
        "c_zeros": np.zeros((128, 128), dtype=np.float32),
        "c_e127": e127,
        "c_masklo8": (fr < p).astype(np.uint8),
        "c_eye16": np.eye(128, dtype=ml_dtypes.bfloat16),
        "c_ones16": np.ones((128, 128), dtype=ml_dtypes.bfloat16),
        "c_up016": (fr >= p).astype(ml_dtypes.bfloat16),
        "c_e12716": e127.astype(ml_dtypes.bfloat16),
        "c_maskup8": (fr > p).astype(np.uint8),
        "c_maskup08": (fr >= p).astype(np.uint8),
    }


CONST_NAMES = ["c_eye", "c_ones", "c_zeros"]
CONST_NAMES_U8 = ["c_masklo8", "c_maskup8", "c_maskup08"]
CONST_NAMES_16 = ["c_eye16", "c_ones16", "c_up016", "c_e12716"]


class Bufs:
    def __init__(self, nc, stack):
        def sb(name, shape, dt=F32):
            return stack.enter_context(nc.sbuf_tensor(name, shape, dt))

        self.khat = sb("sc_khat", [128, T], BF16)
        self.qhat = sb("sc_qhat", [128, T], BF16)
        self.laf = sb("sc_laf", [128, T])
        self.las = sb("sc_las", [128, T])
        self.rowp = sb("sc_rowp", [1, 2048])
        self.rowp16 = sb("sc_rowp16", [1, 1536], BF16)
        self.gate_row = self.rowp
        self.gate16 = self.rowp16
        self.zw = sb("sc_zw", [128, T], BF16)
        self.pw = sb("sc_pw", [128, 3 * NL])
        self.vc_tm = sb("sc_vctm", [128, T], BF16)
        self.eye = sb("sbc_eye", [128, 128])
        self.ones = sb("sbc_ones", [128, 128])
        self.zeros = sb("sbc_zeros", [128, 128])
        self.eye16 = sb("sbc_eye16", [128, 128], BF16)
        self.ones16 = sb("sbc_ones16", [128, 128], BF16)
        self.up016 = sb("sbc_up016", [128, 128], BF16)
        self.e12716 = sb("sbc_e12716", [128, 1], BF16)
        self.masklo8 = sb("sbc_masklo8", [128, 128], U8)
        self.maskup8 = sb("sbc_maskup8", [128, 128], U8)
        self.maskup08 = sb("sbc_maskup08", [128, 128], U8)
        for st in ("f", "s"):
            for nm in ("kt", "qth", "qtp", "ct2", "tt", "aqc", "tat", "aqp"):
                setattr(self, f"{nm}_{st}", sb(f"sc_{nm}{st}", [128, T], BF16))
        self.dend_f = sb("sc_dendf", [128, NL])
        self.dend_s = sb("sc_dends", [128, NL])
        self.eh_f = sb("sc_ehf", [128, NL])
        self.eh_s = sb("sc_ehs", [128, NL])
        self.gcol = sb("sc_gcol", [128, NL])
        for st_ in ("f", "s"):
            setattr(self, f"lam_{st_}", sb(f"sc_lam_{st_}", [128, 128]))
            setattr(self, f"t1_{st_}", sb(f"sc_t1_{st_}", [128, 128]))
            setattr(self, f"t2_{st_}", sb(f"sc_t2_{st_}", [128, 128]))
            setattr(self, f"hcol_{st_}", sb(f"sc_hcol_{st_}", [128, 4]))
        for p_ in (0, 1):
            for st_ in ("f", "s"):
                setattr(self, f"ct16_{p_}{st_}",
                        sb(f"sc_ct16_{p_}{st_}", [128, 128], BF16))
                setattr(self, f"lraw{p_}{st_}",
                        sb(f"sc_lraw{p_}{st_}", [128, 128], BF16))
                setattr(self, f"zh0_{p_}{st_}",
                        sb(f"sc_zh0_{p_}{st_}", [128, 128], BF16))
                setattr(self, f"zh1_{p_}{st_}",
                        sb(f"sc_zh1_{p_}{st_}", [128, 128], BF16))
            setattr(self, f"w1_{p_}", sb(f"sc_w1_{p_}", [128, 128]))
            setattr(self, f"lgat{p_}", sb(f"sc_lgat{p_}", [128, 128], BF16))
        self.g0f = [sb("sc_g0f0", [128, 128]), sb("sc_g0f1", [128, 128])]
        self.g0s = [sb("sc_g0s0", [128, 128]), sb("sc_g0s1", [128, 128])]
        self.g0h16f = sb("sc_g0hf", [128, 128], BF16)
        self.g0h16s = sb("sc_g0hs", [128, 128], BF16)
        self.bv16f = sb("sc_bvf", [128, 128], BF16)
        self.bv16s = sb("sc_bvs", [128, 128], BF16)
        self.m0w = sb("sc_m0w", [128, 256], BF16)
        self.dm16 = sb("sc_dm16", [128, 128], BF16)
        self.m1f = sb("sc_m1f", [128, 128], BF16)
        self.m1s = sb("sc_m1s", [128, 128], BF16)
        self.yf = sb("sc_yf", [128, 128], BF16)
        self.ys = sb("sc_ys", [128, 128], BF16)
        self.dbuf = sb("sc_dbuf", [128, 128])
        self.sqs = self.dbuf
        self.scol = sb("sc_scol", [128, 8])
        self.pcol = sb("sc_pcol", [128, 2])
        self.pcol16 = sb("sc_pcol16", [128, 1], BF16)
        self.ns0r = [sb("sc_ns0r0", [1, 128], BF16),
                     sb("sc_ns0r1", [1, 128], BF16)]
        self.pcol8 = sb("sc_pcol8", [128, NL])
        self.ipcol8 = sb("sc_ipcol8", [128, NL])
        self.scol8 = sb("sc_scol8", [128, NL])
        self.ucol8 = sb("sc_ucol8", [128, NL])
        self.stinit = sb("sc_stinit", [128, 256])
        self.nsc = sb("sc_nsc", [128, T], BF16)
        self.out_tm = sb("sc_outtm", [128, T], BF16)
        self.o_vm = sb("sc_ovm", [128, T], BF16)
        self.nrmw = sb("sc_nrmw", [128, 128])
        self.rmsc = sb("sc_rmsc", [128, 2])

        self.big0a = sb("k_big0a", [128, 8192], BF16)
        self.big0b = sb("k_big0b", [128, 8192], BF16)
        self.xn_fm = sb("k_xnfm", [128, NF * 512], BF16)
        self.hid16 = sb("k_hid16", [128, NH * 512], BF16)
        self.xa = [sb("k_xa0", [128, 1024]), sb("k_xa1", [128, 1024])]
        self.wo16 = sb("k_wo16", [128, 2048], BF16)
        self.ppw16 = sb("k_ppw16", [128, 48], BF16)
        self.nm1_16 = sb("k_nm1", [128, 256], BF16)
        self.nm2_16 = sb("k_nm2", [128, 256], BF16)
        self.nmb1c = sb("k_nmb1c", [128, 2])
        self.nmb2r16 = sb("k_nmb2r", [1, 128], BF16)
        self.bafc = sb("k_bafc", [128, 1])
        self.basc = sb("k_basc", [128, 1])
        self.la8f = sb("k_la8f", [128, 1])
        self.la8s = sb("k_la8s", [128, 1])
        self.irb1c = sb("k_irb1c", [128, 1])
        self.irw2_16 = sb("k_irw2", [128, 1], BF16)
        self.irb2c = sb("k_irb2c", [1, 1])
        self.pwbc = sb("k_pwbc", [128, 1])
        self.nw16r = sb("k_nw16r", [1, 128], BF16)
        self.lncols = sb("k_lncols", [128, 32])
        self.pabs = sb("k_pabs", [128, 4])
        self.grow = sb("k_grow", [1, 4])
        self.grow2 = sb("k_grow2", [1, 4])
        self.throw = sb("k_throw", [1, 4])
        self.thc = sb("k_thc", [128, 3])
        self.nthc = sb("k_nthc", [128, 3])
        self.prj_sq = sb("k_prjsq", [128, 512])
        self.girs16 = sb("k_girs16", [128, 512], BF16)
        self.pwl16 = sb("k_pwl16", [3, T], BF16)
        self.nsab = sb("k_nsab", [1, 132])
        self.pwtmp = sb("k_pwtmp", [128, 3])
        self.pwsc = sb("k_pwsc", [128, 2])
        self.w3buf = [sb("k_w3b0", [128, W3G * 512], BF16),
                      sb("k_w3b1", [128, W3G * 512], BF16)]
        self.stload = sb("k_stload", [128, 512])
        self.spack = sb("k_spack", [128, 256])
        self.rsel_sb = sb("k_rsel", [128, 8])
        self.rselq_sb = sb("k_rselq", [128, 8])
        self.rsel2_sb = sb("k_rsel2", [1, 8])
        self.deh16 = self.yf
        self.tch16 = self.dm16
        self.pjt16 = self.ys
        self.st16f = self.m1f
        self.st16s = self.m1s


def emit_prep_chunk(P, B, psum, j, deps):
    cs = slice(j * 128, (j + 1) * 128)
    jc = slice(j, j + 1)
    p_ = j % 2
    w1 = getattr(B, f"w1_{p_}")
    lgat = getattr(B, f"lgat{p_}")
    pA, pB = psum["pA"], psum["pB"]

    P.op("t", lambda e: e.matmul(
        pB[:, 384:385], B.gate16[0:1, cs], B.ones16[0:1, 0:1],
        start=True, stop=True),
        reads=[B.gate16, B.ones16], writes=[pB], after=deps)
    P.op("v", lambda e: e.tensor_copy(B.gcol[:, jc], pB[:, 384:385]),
         reads=[pB], writes=[B.gcol])
    P.op("t", lambda e: e.matmul(
        pB[:, 256:384], B.ones16[0:1, :], B.gate16[0:1, cs],
        start=True, stop=True),
        reads=[B.ones16, B.gate16], writes=[pB], after=deps)
    P.op("v", lambda e: e.tensor_copy(w1[:], pB[:, 256:384]),
         reads=[pB], writes=[w1])

    def one_state(st):
        la_buf = B.laf if st == "f" else B.las
        K_H = K_HORNER_F if st == "f" else K_HORNER_S
        kt = getattr(B, f"kt_{st}")
        qth = getattr(B, f"qth_{st}")
        qtp = getattr(B, f"qtp_{st}")
        ct2 = getattr(B, f"ct2_{st}")
        tt = getattr(B, f"tt_{st}")
        aqc = getattr(B, f"aqc_{st}")
        tat = getattr(B, f"tat_{st}")
        aqp = getattr(B, f"aqp_{st}")
        dend = getattr(B, f"dend_{st}")
        eh = getattr(B, f"eh_{st}")
        lam = getattr(B, f"lam_{st}")
        t1 = getattr(B, f"t1_{st}")
        t2 = getattr(B, f"t2_{st}")
        hcol = getattr(B, f"hcol_{st}")
        ct16 = getattr(B, f"ct16_{p_}{st}")
        lraw = getattr(B, f"lraw{p_}{st}")
        zh = [getattr(B, f"zh0_{p_}{st}"), getattr(B, f"zh1_{p_}{st}")]

        P.op("v", lambda e, la=la_buf: e.tensor_tensor_scan(
            lam[:], la[:, cs], B.zeros[:], 0.0, op0=ALU.add, op1=ALU.add),
            reads=[la_buf, B.zeros], writes=[lam], after=deps)
        P.op("s", lambda e, d=dend: e.activation(
            d[:, jc], lam[:, 127:128], AF.Exp), reads=[lam], writes=[dend])
        P.op("s", lambda e, h=eh: e.activation(
            h[:, jc], lam[:, 127:128], AF.Exp, scale=f32(0.5)),
            reads=[lam], writes=[eh])
        P.op("s", lambda e: e.mul(hcol[:, 0:1], lam[:, 127:128], f32(-0.5)),
             reads=[lam], writes=[hcol])
        P.op("s", lambda e: e.mul(hcol[:, 1:2], lam[:, 127:128], f32(0.5)),
             reads=[lam], writes=[hcol])
        P.op("v", lambda e, la=la_buf: e.tensor_tensor(
            t1[:], lam[:], la[:, cs], op=ALU.subtract),
            reads=[lam, la_buf], writes=[t1])
        P.op("s", lambda e: e.activation(
            t1[:], t1[:], AF.Exp, bias=hcol[:, 0:1], scale=f32(1.0)),
            reads=[t1, hcol], writes=[t1])
        P.op("v", lambda e, o=kt: e.tensor_tensor(
            o[:, cs], t1[:], B.khat[:, cs], op=ALU.mult),
            reads=[t1, B.khat], writes=[kt], after=deps)
        P.op("v", lambda e, o=qth: e.tensor_tensor(
            o[:, cs], t1[:], B.qhat[:, cs], op=ALU.mult),
            reads=[t1, B.qhat], writes=[qth], after=deps)
        P.op("s", lambda e, la=la_buf: e.activation(
            t2[:], la[:, cs], AF.Exp), reads=[la_buf], writes=[t2],
            after=deps)
        P.op("v", lambda e: e.tensor_tensor(
            t1[:], t1[:], t2[:], op=ALU.mult), reads=[t1, t2], writes=[t1])
        P.op("v", lambda e, o=qtp: e.tensor_tensor(
            o[:, cs], t1[:], B.qhat[:, cs], op=ALU.mult),
            reads=[t1, B.qhat], writes=[qtp], after=deps)
        P.op("v", lambda e: e.tensor_scalar(
            t2[:], t2[:], -1.0, -1.0,
            op0=ALU.mult, op1=ALU.subtract), reads=[t2], writes=[t2])
        P.op("s", lambda e: e.activation(
            t1[:], lam[:], AF.Exp, bias=hcol[:, 1:2], scale=f32(-1.0)),
            reads=[lam, hcol], writes=[t1])
        P.op("v", lambda e: e.tensor_tensor(
            t1[:], t1[:], t2[:], op=ALU.mult), reads=[t1, t2], writes=[t1])
        P.op("v", lambda e: e.tensor_tensor(
            t1[:], t1[:], B.khat[:, cs], op=ALU.mult),
            reads=[t1, B.khat], writes=[t1], after=deps)
        P.op("v", lambda e: e.tensor_copy(ct16[:], t1[:]),
             reads=[t1], writes=[ct16])
        P.op("v", lambda e, h=eh: e.tensor_single_scalar(
            t1[:], t1[:], h[:, jc], op=ALU.mult), reads=[t1, eh], writes=[t1])
        P.op("t", lambda e: e.transpose(pB[:, 0:128], t1[:], B.eye[:]),
             reads=[t1, B.eye], writes=[pB])
        P.op("v", lambda e, o=ct2: e.tensor_copy(o[:, cs], pB[:, 0:128]),
             reads=[pB], writes=[ct2])
        P.op("t", lambda e, a=kt: e.matmul(
            pA[:, 0:128], a[:, cs], ct16[:], start=True, stop=True),
            reads=[kt, ct16], writes=[pA])
        P.op("v", lambda e: e.select(
            lraw[:], B.masklo8[:], pA[:, 0:128], B.zeros[:], add_drain=True),
            reads=[B.masklo8, pA, B.zeros], writes=[lraw])
        if st == "s":
            P.op("v", lambda e: e.tensor_tensor(
                lgat[:], lraw[:], w1[:], op=ALU.mult),
                reads=[lraw, w1], writes=[lgat])
            lmat = lgat
        else:
            lmat = lraw
        P.op("t", lambda e, a=qth: e.matmul(
            pA[:, 128:256], ct16[:], a[:, cs], start=True, stop=True),
            reads=[ct16, qth], writes=[pA])
        P.op("v", lambda e, o=aqc: e.select(
            o[:, cs], B.maskup8[:], pA[:, 128:256], B.zeros[:],
            add_drain=True),
            reads=[B.maskup8, pA, B.zeros], writes=[aqc])
        P.op("t", lambda e, a=qtp: e.matmul(
            pA[:, 256:384], ct16[:], a[:, cs], start=True, stop=True),
            reads=[ct16, qtp], writes=[pA])
        P.op("v", lambda e, o=aqp: e.select(
            o[:, cs], B.maskup08[:], pA[:, 256:384], B.zeros[:],
            add_drain=True),
            reads=[B.maskup08, pA, B.zeros], writes=[aqp])
        if st == "s":
            P.op("v", lambda e, o=aqc: e.tensor_single_scalar(
                o[:, cs], o[:, cs], B.gcol[:, jc], op=ALU.mult),
                reads=[aqc, B.gcol], writes=[aqc])
            P.op("v", lambda e, o=aqp: e.tensor_single_scalar(
                o[:, cs], o[:, cs], B.gcol[:, jc], op=ALU.mult),
                reads=[aqp, B.gcol], writes=[aqp])
        preg = slice(128 * (1 + p_), 128 * (2 + p_))
        P.op("v", lambda e, lm=lmat: e.tensor_tensor(
            zh[0][:], B.eye[:], lm[:], op=ALU.subtract),
            reads=[B.eye, lmat], writes=[zh[0]])
        cur, nxt = 0, 1
        for _ in range(K_H - 1):
            P.op("t", lambda e, lm=lmat, c=cur: e.matmul(
                pB[:, preg], lm[:], zh[c][:], start=True, stop=True),
                reads=[lmat, zh[cur]], writes=[pB])
            P.op("v", lambda e, n=nxt: e.tensor_tensor(
                zh[n][:], B.eye[:], pB[:, preg], op=ALU.subtract),
                reads=[B.eye, pB], writes=[zh[nxt]])
            cur, nxt = nxt, cur
        P.op("v", lambda e, o=tt, c=cur: e.tensor_copy(o[:, cs], zh[c][:]),
             reads=[zh[cur]], writes=[tt])
        P.op("t", lambda e, o=tt: e.matmul(
            pB[:, preg], lraw[:], o[:, cs], start=True, stop=True),
            reads=[lraw, tt], writes=[pB])
        P.op("v", lambda e, o=tat: e.tensor_copy(o[:, cs], pB[:, preg]),
             reads=[pB], writes=[tat])

    one_state("f")
    one_state("s")


def emit_relay_lite(P, B, psum, j, state_deps=()):
    cs = slice(j * 128, (j + 1) * 128)
    jc = slice(j, j + 1)
    cur, nxt = j % 2, 1 - j % 2
    g0f_c, g0f_n = B.g0f[cur], B.g0f[nxt]
    g0s_c, g0s_n = B.g0s[cur], B.g0s[nxt]
    pR, pG, pO = psum["pR"], psum["pG"], psum["pO"]
    sd = list(state_deps)

    P.op("v", lambda e: e.tensor_single_scalar(
        B.g0h16f[:], g0f_c[:], B.eh_f[:, jc], op=ALU.mult),
        reads=[g0f_c, B.eh_f], writes=[B.g0h16f], after=sd)
    P.op("v", lambda e: e.tensor_single_scalar(
        B.g0h16s[:], g0s_c[:], B.eh_s[:, jc], op=ALU.mult),
        reads=[g0s_c, B.eh_s], writes=[B.g0h16s], after=sd)
    P.op("t", lambda e: e.matmul(
        pR[:, 0:128], B.kt_f[:, cs], B.g0h16f[:], start=True, stop=True),
        reads=[B.kt_f, B.g0h16f], writes=[pR])
    P.op("t", lambda e: e.matmul(
        pR[:, 128:256], B.kt_s[:, cs], B.g0h16s[:], start=True, stop=True),
        reads=[B.kt_s, B.g0h16s], writes=[pR])
    P.op("v", lambda e: e.tensor_tensor(
        B.bv16f[:], B.vc_tm[:, cs], pR[:, 0:128], op=ALU.subtract),
        reads=[B.vc_tm, pR], writes=[B.bv16f])
    P.op("v", lambda e: e.tensor_tensor(
        B.bv16s[:], B.vc_tm[:, cs], pR[:, 128:256], op=ALU.subtract),
        reads=[B.vc_tm, pR], writes=[B.bv16s])
    P.op("t", lambda e: e.matmul(
        pG[:, 0:128], B.tt_f[:, cs], B.bv16f[:], start=True, stop=True),
        reads=[B.tt_f, B.bv16f], writes=[pG])
    P.op("t", lambda e: e.matmul(
        pG[:, 128:256], B.tt_s[:, cs], B.bv16s[:], start=True, stop=True),
        reads=[B.tt_s, B.bv16s], writes=[pG])
    P.op("v", lambda e: e.tensor_copy(B.m0w[:], pG[:, 0:256]),
         reads=[pG], writes=[B.m0w])
    P.op("t", lambda e: e.matmul(
        pG[:, 256:384], B.ct2_f[:, cs], B.m0w[:, 0:128],
        start=True, stop=True),
        reads=[B.ct2_f, B.m0w], writes=[pG])
    P.op("v", lambda e: e.scalar_tensor_tensor(
        g0f_n[:], g0f_c[:], B.dend_f[:, jc], pG[:, 256:384],
        op0=ALU.mult, op1=ALU.add),
        reads=[g0f_c, B.dend_f, pG], writes=[g0f_n])
    P.op("t", lambda e: e.matmul(
        pG[:, 384:512], B.ct2_s[:, cs], B.m0w[:, 128:256],
        start=True, stop=True),
        reads=[B.ct2_s, B.m0w], writes=[pG])
    P.op("v", lambda e: e.scalar_tensor_tensor(
        g0s_n[:], g0s_c[:], B.dend_s[:, jc], pG[:, 384:512],
        op0=ALU.mult, op1=ALU.add),
        reads=[g0s_c, B.dend_s, pG], writes=[g0s_n])

    P.op("t", lambda e: e.matmul(
        pR[:, 256:384], B.qth_f[:, cs], B.g0h16f[:], start=True, stop=False),
        reads=[B.qth_f, B.g0h16f], writes=[pR])
    P.op("t", lambda e: e.matmul(
        pR[:, 256:384], B.qth_s[:, cs], B.g0h16s[:], start=False, stop=False),
        reads=[B.qth_s, B.g0h16s], writes=[pR])
    P.op("t", lambda e: e.matmul(
        pR[:, 256:384], B.aqc_f[:, cs], B.m0w[:, 0:128],
        start=False, stop=False),
        reads=[B.aqc_f, B.m0w], writes=[pR])
    P.op("t", lambda e: e.matmul(
        pR[:, 256:384], B.aqc_s[:, cs], B.m0w[:, 128:256],
        start=False, stop=True),
        reads=[B.aqc_s, B.m0w], writes=[pR])
    P.op("v", lambda e: e.scalar_tensor_tensor(
        B.dbuf[:], pR[:, 256:384], -0.5, B.vc_tm[:, cs],
        op0=ALU.mult, op1=ALU.add), reads=[pR, B.vc_tm], writes=[B.dbuf])
    P.op("s", lambda e: e.activation(
        B.sqs[:], B.dbuf[:], AF.Square, accum_out=B.scol[:, 0:1]),
        reads=[B.dbuf], writes=[B.sqs, B.scol])
    P.op("s", lambda e: e.activation(
        B.scol[:, 1:2], B.scol[:, 0:1], AF.Exp,
        scale=f32(-1.0 / (1.0 + 1e-6))), reads=[B.scol], writes=[B.scol])
    P.op("v", lambda e: e.tensor_single_scalar(
        B.scol[:, 1:2], B.scol[:, 1:2], 1.0, op=ALU.add),
        reads=[B.scol], writes=[B.scol])
    P.op("v", lambda e: e.reciprocal(B.scol8[:, jc], B.scol[:, 1:2]),
        reads=[B.scol], writes=[B.scol8])
    P.op("v", lambda e: e.tensor_single_scalar(
        B.ucol8[:, jc], B.scol8[:, jc], 0.7, op=ALU.is_gt),
        reads=[B.scol8], writes=[B.ucol8])
    P.op("v", lambda e: e.tensor_single_scalar(
        B.scol[:, 6:7], B.ucol8[:, jc], float(np.log(0.9)), op=ALU.mult),
        reads=[B.ucol8], writes=[B.scol])
    P.op("v", lambda e: e.tensor_copy(B.pcol16[:, 0:1], B.scol[:, 6:7]),
         reads=[B.scol], writes=[B.pcol16])
    P.op("t", lambda e: e.matmul(
        pO[:, 384:385], B.up016[:], B.pcol16[:, 0:1], start=True, stop=True),
        reads=[B.up016, B.pcol16], writes=[pO])
    P.op("s", lambda e: e.activation(
        B.pcol8[:, jc], pO[:, 384:385], AF.Exp), reads=[pO],
        writes=[B.pcol8])
    P.op("s", lambda e: e.activation(
        B.ipcol8[:, jc], pO[:, 384:385], AF.Exp, scale=f32(-1.0)),
        reads=[pO], writes=[B.ipcol8])
    P.op("t", lambda e: e.matmul(
        pO[0:1, 255:256], B.pcol16[:, 0:1], B.ones16[:, 0:1],
        start=True, stop=True),
        reads=[B.pcol16, B.ones16], writes=[pO])
    P.op("s", lambda e: e.activation(
        B.nsab[0:1, 1:2], pO[0:1, 255:256], AF.Exp),
        reads=[pO], writes=[B.nsab])
    P.op("v", lambda e: e.tensor_tensor(
        B.scol[:, 7:8], B.ucol8[:, jc], B.ipcol8[:, jc], op=ALU.mult),
        reads=[B.ucol8, B.ipcol8], writes=[B.scol])
    P.op("v", lambda e: e.tensor_single_scalar(
        B.zw[:, cs], B.zw[:, cs], B.scol[:, 7:8], op=ALU.mult),
        reads=[B.zw, B.scol], writes=[B.zw])
    P.op("t", lambda e: e.matmul(
        pO[0:1, 256:384], B.ones16[:, 0:1], B.zw[:, cs],
        start=True, stop=True),
        reads=[B.ones16, B.zw], writes=[pO])
    P.op("v", lambda e: e.tensor_single_scalar(
        B.nsab[0:1, 0:1], B.nsab[0:1, 0:1], B.nsab[0:1, 1:2], op=ALU.mult),
        reads=[B.nsab], writes=[B.nsab])
    P.op("v", lambda e: e.tensor_tensor(
        B.nsab[0:1, 4:132], B.nsab[0:1, 4:132], pO[0:1, 256:384],
        op=ALU.add), reads=[B.nsab, pO], writes=[B.nsab])
    P.op("v", lambda e: e.tensor_single_scalar(
        B.nsab[0:1, 4:132], B.nsab[0:1, 4:132], B.nsab[0:1, 1:2],
        op=ALU.mult), reads=[B.nsab], writes=[B.nsab])


def emit_tchain(P, B, pb, j):
    cs = slice(j * 128, (j + 1) * 128)
    jc = slice(j, j + 1)
    for si, st in enumerate(("f", "s")):
        kt = getattr(B, "kt_" + st)
        tt = getattr(B, "tt_" + st)
        ct2 = getattr(B, "ct2_" + st)
        eh = getattr(B, "eh_" + st)
        dend = getattr(B, "dend_" + st)
        ta = slice(128 * (si + 1), 128 * (si + 2))
        t16 = slice(128 * si, 128 * (si + 1))
        P.op("v", lambda e, eh=eh: e.tensor_single_scalar(
            B.deh16[:], B.eye[:], eh[:, jc], op=ALU.mult),
            reads=[B.eye, eh], writes=[B.deh16])
        P.op("t", lambda e, kt=kt: e.matmul(
            pb[0][:, 0:128], kt[:, cs], B.deh16[:], start=True, stop=True),
            reads=[kt, B.deh16], writes=[pb[0]])
        P.op("v", lambda e: e.tensor_copy(B.tch16[:], pb[0][:, 0:128]),
             reads=[pb[0]], writes=[B.tch16])
        P.op("t", lambda e, tt=tt: e.matmul(
            pb[0][:, 128:256], tt[:, cs], B.tch16[:], start=True, stop=True),
            reads=[tt, B.tch16], writes=[pb[0]])
        P.op("v", lambda e: e.tensor_copy(B.tch16[:], pb[0][:, 128:256]),
             reads=[pb[0]], writes=[B.tch16])
        P.op("t", lambda e, ct2=ct2: e.matmul(
            pb[0][:, 256:384], ct2[:, cs], B.tch16[:], start=True, stop=True),
            reads=[ct2, B.tch16], writes=[pb[0]])
        if j == 0:
            P.op("v", lambda e, dend=dend, ta=ta: e.scalar_tensor_tensor(
                B.prj_sq[:, ta], B.eye[:], dend[:, jc], pb[0][:, 256:384],
                op0=ALU.mult, op1=ALU.subtract),
                reads=[B.eye, dend, pb[0]], writes=[B.prj_sq])
        else:
            P.op("v", lambda e: e.tensor_copy(
                B.prj_sq[:, 0:128], pb[0][:, 256:384]),
                reads=[pb[0]], writes=[B.prj_sq])
            P.op("t", lambda e: e.transpose(
                pb[0][:, 384:512], B.prj_sq[:, 0:128], B.eye[:]),
                reads=[B.prj_sq, B.eye], writes=[pb[0]])
            P.op("v", lambda e: e.tensor_copy(B.pjt16[:], pb[0][:, 384:512]),
                 reads=[pb[0]], writes=[B.pjt16])
            P.op("t", lambda e, t16=t16: e.matmul(
                pb[1][:, 0:128], B.pjt16[:], B.girs16[:, t16],
                start=True, stop=True),
                reads=[B.pjt16, B.girs16], writes=[pb[1]])
            P.op("v", lambda e, dend=dend, ta=ta: e.scalar_tensor_tensor(
                B.prj_sq[:, ta], B.prj_sq[:, ta], dend[:, jc],
                pb[1][:, 0:128], op0=ALU.mult, op1=ALU.subtract),
                reads=[B.prj_sq, dend, pb[1]], writes=[B.prj_sq])
        P.op("v", lambda e, ta=ta, t16=t16: e.tensor_copy(
            B.girs16[:, t16], B.prj_sq[:, ta]),
            reads=[B.prj_sq], writes=[B.girs16])


def emit_replay_chunk(P, B, psum, j, state_deps=()):
    cs = slice(j * 128, (j + 1) * 128)
    jc = slice(j, j + 1)
    cur, nxt = j % 2, 1 - j % 2
    g0f_c, g0f_n = B.g0f[cur], B.g0f[nxt]
    g0s_c, g0s_n = B.g0s[cur], B.g0s[nxt]
    ns_c, ns_n = B.ns0r[cur], B.ns0r[nxt]
    pR, pG, pO = psum["pR"], psum["pG"], psum["pO"]
    sd = list(state_deps)

    P.op("v", lambda e: e.tensor_single_scalar(
        B.g0h16f[:], g0f_c[:], B.eh_f[:, jc], op=ALU.mult),
        reads=[g0f_c, B.eh_f], writes=[B.g0h16f], after=sd)
    P.op("v", lambda e: e.tensor_single_scalar(
        B.g0h16s[:], g0s_c[:], B.eh_s[:, jc], op=ALU.mult),
        reads=[g0s_c, B.eh_s], writes=[B.g0h16s], after=sd)
    P.op("t", lambda e: e.matmul(
        pR[:, 0:128], B.kt_f[:, cs], B.g0h16f[:], start=True, stop=True),
        reads=[B.kt_f, B.g0h16f], writes=[pR])
    P.op("t", lambda e: e.matmul(
        pR[:, 128:256], B.kt_s[:, cs], B.g0h16s[:], start=True, stop=True),
        reads=[B.kt_s, B.g0h16s], writes=[pR])
    P.op("v", lambda e: e.tensor_tensor(
        B.bv16f[:], B.vc_tm[:, cs], pR[:, 0:128], op=ALU.subtract),
        reads=[B.vc_tm, pR], writes=[B.bv16f])
    P.op("v", lambda e: e.tensor_tensor(
        B.bv16s[:], B.vc_tm[:, cs], pR[:, 128:256], op=ALU.subtract),
        reads=[B.vc_tm, pR], writes=[B.bv16s])
    P.op("t", lambda e: e.matmul(
        pG[:, 0:128], B.tt_f[:, cs], B.bv16f[:], start=True, stop=True),
        reads=[B.tt_f, B.bv16f], writes=[pG])
    P.op("v", lambda e: e.tensor_copy(B.m0w[:, 0:128], pG[:, 0:128]),
         reads=[pG], writes=[B.m0w])
    P.op("t", lambda e: e.matmul(
        pG[:, 128:256], B.tt_s[:, cs], B.bv16s[:], start=True, stop=True),
        reads=[B.tt_s, B.bv16s], writes=[pG])
    P.op("v", lambda e: e.tensor_copy(B.m0w[:, 128:256], pG[:, 128:256]),
         reads=[pG], writes=[B.m0w])
    P.op("t", lambda e: e.matmul(
        pR[:, 256:384], B.qth_f[:, cs], B.g0h16f[:], start=True, stop=False),
        reads=[B.qth_f, B.g0h16f], writes=[pR])
    P.op("t", lambda e: e.matmul(
        pR[:, 256:384], B.qth_s[:, cs], B.g0h16s[:], start=False, stop=False),
        reads=[B.qth_s, B.g0h16s], writes=[pR])
    P.op("t", lambda e: e.matmul(
        pR[:, 256:384], B.aqc_f[:, cs], B.m0w[:, 0:128], start=False, stop=False),
        reads=[B.aqc_f, B.m0w], writes=[pR])
    P.op("t", lambda e: e.matmul(
        pR[:, 256:384], B.aqc_s[:, cs], B.m0w[:, 128:256], start=False, stop=True),
        reads=[B.aqc_s, B.m0w], writes=[pR])
    P.op("v", lambda e: e.tensor_copy(B.scol[:, 1:2], B.scol8[:, jc]),
         reads=[B.scol8], writes=[B.scol])
    P.op("v", lambda e: e.tensor_single_scalar(
        B.scol[:, 2:3], B.scol[:, 1:2], 1.0, op=ALU.subtract),
        reads=[B.scol], writes=[B.scol])
    P.op("v", lambda e: e.tensor_tensor(
        B.scol[:, 3:4], B.scol[:, 2:3], B.gcol[:, jc], op=ALU.mult),
        reads=[B.scol, B.gcol], writes=[B.scol])
    P.op("v", lambda e: e.tensor_tensor(
        B.scol[:, 4:5], B.scol[:, 1:2], B.gcol[:, jc], op=ALU.mult),
        reads=[B.scol, B.gcol], writes=[B.scol])
    P.op("v", lambda e: e.tensor_single_scalar(
        B.dm16[:], B.m0w[:, 0:128], B.scol[:, 2:3], op=ALU.mult),
        reads=[B.m0w, B.scol], writes=[B.dm16])
    P.op("t", lambda e: e.matmul(
        pG[:, 256:384], B.tat_f[:, cs], B.dm16[:], start=True, stop=True),
        reads=[B.tat_f, B.dm16], writes=[pG])
    P.op("v", lambda e: e.tensor_tensor(
        B.m1f[:], B.m0w[:, 0:128], pG[:, 256:384], op=ALU.subtract),
        reads=[B.m0w, pG], writes=[B.m1f])
    P.op("v", lambda e: e.tensor_single_scalar(
        B.yf[:], B.m1f[:], B.scol[:, 1:2], op=ALU.mult),
        reads=[B.m1f, B.scol], writes=[B.yf])
    P.op("v", lambda e: e.tensor_single_scalar(
        B.dm16[:], B.m0w[:, 128:256], B.scol[:, 3:4], op=ALU.mult),
        reads=[B.m0w, B.scol], writes=[B.dm16])
    P.op("t", lambda e: e.matmul(
        pG[:, 256:384], B.tat_s[:, cs], B.dm16[:], start=True, stop=True),
        reads=[B.tat_s, B.dm16], writes=[pG])
    P.op("v", lambda e: e.tensor_tensor(
        B.m1s[:], B.m0w[:, 128:256], pG[:, 256:384], op=ALU.subtract),
        reads=[B.m0w, pG], writes=[B.m1s])
    P.op("v", lambda e: e.tensor_single_scalar(
        B.ys[:], B.m1s[:], B.scol[:, 4:5], op=ALU.mult),
        reads=[B.m1s, B.scol], writes=[B.ys])
    P.op("t", lambda e: e.matmul(
        pG[:, 0:128], B.ct2_f[:, cs], B.yf[:], start=True, stop=True),
        reads=[B.ct2_f, B.yf], writes=[pG])
    P.op("v", lambda e: e.scalar_tensor_tensor(
        g0f_n[:], g0f_c[:], B.dend_f[:, jc], pG[:, 0:128],
        op0=ALU.mult, op1=ALU.add),
        reads=[g0f_c, B.dend_f, pG], writes=[g0f_n])
    P.op("t", lambda e: e.matmul(
        pG[:, 128:256], B.ct2_s[:, cs], B.ys[:], start=True, stop=True),
        reads=[B.ct2_s, B.ys], writes=[pG])
    P.op("v", lambda e: e.scalar_tensor_tensor(
        g0s_n[:], g0s_c[:], B.dend_s[:, jc], pG[:, 128:256],
        op0=ALU.mult, op1=ALU.add),
        reads=[g0s_c, B.dend_s, pG], writes=[g0s_n])

    P.op("v", lambda e, r=ns_c: e.tensor_copy(r[0:1, :], B.nsab[0:1, 4:132]),
         reads=[B.nsab], writes=[ns_c])
    P.op("t", lambda e: e.matmul(
        pO[:, 256:384], B.up016[:], B.zw[:, cs], start=True, stop=False),
        reads=[B.up016, B.zw], writes=[pO])
    P.op("t", lambda e, r=ns_c: e.matmul(
        pO[:, 256:384], B.ones16[0:1, :], r[0:1, :], start=False, stop=True),
        reads=[B.ones16, ns_c], writes=[pO], after=sd)
    tnsc = P.op("v", lambda e: e.tensor_single_scalar(
        B.nsc[:, cs], pO[:, 256:384], B.pcol8[:, jc], op=ALU.mult),
        reads=[pO, B.pcol8], writes=[B.nsc])
    P.op("v", lambda e: e.tensor_copy(B.pcol16[:, 0:1], B.ucol8[:, jc]),
         reads=[B.ucol8], writes=[B.pcol16])
    P.op("t", lambda e: e.matmul(
        pO[0:1, 255:256], B.pcol16[:, 0:1], B.ones16[:, 0:1],
        start=True, stop=True),
        reads=[B.pcol16, B.ones16], writes=[pO])
    P.op("s", lambda e: e.activation(
        B.nsab[0:1, 1:2], pO[0:1, 255:256], AF.Exp,
        scale=f32(np.log(0.9))), reads=[pO], writes=[B.nsab])
    P.op("t", lambda e: e.matmul(
        pO[0:1, 112:240], B.ones16[:, 0:1], B.zw[:, cs], start=True,
        stop=True),
        reads=[B.ones16, B.zw], writes=[pO])
    P.op("v", lambda e: e.tensor_tensor(
        B.nsab[0:1, 4:132], B.nsab[0:1, 4:132], pO[0:1, 112:240],
        op=ALU.add), reads=[B.nsab, pO], writes=[B.nsab])
    P.op("v", lambda e: e.tensor_single_scalar(
        B.nsab[0:1, 4:132], B.nsab[0:1, 4:132], B.nsab[0:1, 1:2],
        op=ALU.mult), reads=[B.nsab], writes=[B.nsab])

    P.op("t", lambda e: e.matmul(
        pO[:, 0:128], B.qtp_f[:, cs], B.g0h16f[:], start=True, stop=False),
        reads=[B.qtp_f, B.g0h16f], writes=[pO])
    P.op("t", lambda e: e.matmul(
        pO[:, 0:128], B.aqp_f[:, cs], B.yf[:], start=False, stop=True),
        reads=[B.aqp_f, B.yf], writes=[pO])
    P.op("t", lambda e: e.matmul(
        pO[:, 128:256], B.qtp_s[:, cs], B.g0h16s[:], start=True, stop=False),
        reads=[B.qtp_s, B.g0h16s], writes=[pO])
    P.op("t", lambda e: e.matmul(
        pO[:, 128:256], B.aqp_s[:, cs], B.ys[:], start=False, stop=True),
        reads=[B.aqp_s, B.ys], writes=[pO])
    P.op("v", lambda e: e.tensor_single_scalar(
        B.out_tm[:, cs], pO[:, 0:128], B.pw[:, 3 * j:3 * j + 1],
        op=ALU.mult), reads=[pO, B.pw], writes=[B.out_tm])
    P.op("v", lambda e: e.scalar_tensor_tensor(
        B.out_tm[:, cs], pO[:, 128:256], B.pw[:, 3 * j + 1:3 * j + 2],
        B.out_tm[:, cs], op0=ALU.mult, op1=ALU.add),
        reads=[pO, B.pw, B.out_tm], writes=[B.out_tm])
    P.op("v", lambda e: e.scalar_tensor_tensor(
        B.out_tm[:, cs], B.nsc[:, cs], B.pw[:, 3 * j + 2:3 * j + 3],
        B.out_tm[:, cs], op0=ALU.mult, op1=ALU.add),
        reads=[B.nsc, B.pw, B.out_tm], writes=[B.out_tm])
    P.op("s", lambda e: e.activation(
        B.sqs[:], B.out_tm[:, cs], AF.Square, accum_out=B.rmsc[:, 0:1]),
        reads=[B.out_tm], writes=[B.sqs, B.rmsc])
    P.op("s", lambda e: e.activation(
        B.rmsc[:, 1:2], B.rmsc[:, 0:1], AF.Sqrt,
        scale=f32(1.0 / 128.0), bias=f32(1e-6)),
        reads=[B.rmsc], writes=[B.rmsc])
    P.op("v", lambda e: e.reciprocal(B.rmsc[:, 1:2], B.rmsc[:, 1:2]),
         reads=[B.rmsc], writes=[B.rmsc])
    P.op("v", lambda e: e.tensor_single_scalar(
        B.sqs[:], B.out_tm[:, cs], B.rmsc[:, 1:2], op=ALU.mult),
        reads=[B.out_tm, B.rmsc], writes=[B.sqs])
    P.op("v", lambda e: e.tensor_tensor(
        B.dbuf[:], B.sqs[:], B.nrmw[:], op=ALU.mult),
        reads=[B.sqs, B.nrmw], writes=[B.dbuf])
    P.op("t", lambda e: e.transpose(pO[:, 256:384], B.dbuf[:], B.eye[:]),
         reads=[B.dbuf, B.eye], writes=[pO])
    P.op("v", lambda e: e.tensor_copy(B.o_vm[:, cs], pO[:, 256:384]),
         reads=[pO], writes=[B.o_vm])


_CACHE = {}


def build():
    nc = bass.Bass("TRN2", target_bir_lowering=False)
    consts = np_consts()

    def din(name, shape, dt=F32):
        return nc.dram_tensor(name, shape, dt, kind="ExternalInput").ap()

    x = din("x", [T, D])
    wk = din("wk", [D, 128])
    wv = din("wv", [D, 128])
    wq = din("wq", [D, 128])
    waf = din("waf", [D, 128])
    was = din("was", [D, 128])
    baf = din("baf", [128])
    bas = din("bas", [128])
    flam = din("flam", [128])
    slam = din("slam", [128])
    nmw1 = din("nmw1", [128, 256])
    nmb1 = din("nmb1", [256])
    nmw2 = din("nmw2", [256, 128])
    nmb2 = din("nmb2", [128])
    irw1 = din("irw1", [D, 128])
    irb1 = din("irb1", [128])
    irw2 = din("irw2", [128, 1])
    irb2 = din("irb2", [1])
    pww = din("pww", [D, 3])
    pwb = din("pwb", [3])
    wo = din("wo", [128, D])
    nrm = din("nrm", [128])
    w1s = din("w1s", [D // 8, H])
    w2s = din("w2s", [D // 8, H])
    w3s = din("w3s", [H // 8, D])
    rsel = din("rsel", [128, 8])
    rsel2 = din("rsel2", [1, 8])
    cins = {n: din(n, list(consts[n].shape)) for n in CONST_NAMES}
    for n in CONST_NAMES_U8:
        cins[n] = din(n, [128, 128], U8)
    for n in CONST_NAMES_16:
        cins[n] = din(n, list(consts[n].shape), BF16)

    y_out = nc.dram_tensor("y", [T, D], F32, kind="ExternalOutput").ap()

    xnsc = nc.dram_tensor("xnsc", [T, D], BF16, kind="Internal").ap()
    cc_gin = nc.dram_tensor("cc_gin", [1, 4], F32, kind="Internal").ap()
    cc_gout = nc.dram_tensor("cc_gout", [1, 4], F32, kind="Internal").ap()
    cc_w1in = nc.dram_tensor("cc_w1in", [D // 8, H], BF16,
                             kind="Internal").ap()
    cc_w1out = nc.dram_tensor("cc_w1out", [D, H], BF16, kind="Internal").ap()
    cc_w2in = nc.dram_tensor("cc_w2in", [D // 8, H], BF16,
                             kind="Internal").ap()
    cc_w2out = nc.dram_tensor("cc_w2out", [D, H], BF16, kind="Internal").ap()
    cc_w3in = nc.dram_tensor("cc_w3in", [H // 8, D], BF16,
                             kind="Internal").ap()
    cc_w3out = nc.dram_tensor("cc_w3out", [H, D], BF16, kind="Internal").ap()
    cc_rin = nc.dram_tensor("cc_rin", [128, 512], F32, kind="Internal").ap()
    cc_rout = nc.dram_tensor("cc_rout", [1024, 512], F32,
                             kind="Internal").ap()
    cc_nsin = nc.dram_tensor("cc_nsin", [1, 132], F32, kind="Internal").ap()
    cc_nsout = nc.dram_tensor("cc_nsout", [8, 132], F32,
                              kind="Internal").ap()

    with ExitStack() as stack:
        B = Bufs(nc, stack)
        P = Prog()
        pb = []
        for i in range(7):
            h = stack.enter_context(nc.psum_tensor(f"pp{i}", [128, 512], F32))
            P.mark_excl(h)
            pb.append(h)
        h = stack.enter_context(nc.psum_tensor("pp7", [128, 512], F32))
        P.mark_excl(h)
        pb.append(h)
        psum = {"pA": pb[4], "pB": pb[5], "pR": pb[6], "pG": pb[7],
                "pO": pb[4]}
        sems = {k: stack.enter_context(nc.semaphore("sem_" + k))
                for k in Prog.ENGINES}
        sems["ya"] = stack.enter_context(nc.semaphore("sem_ya"))
        sems["yb"] = stack.enter_context(nc.semaphore("sem_yb"))
        sems["za"] = stack.enter_context(nc.semaphore("sem_za"))
        sems["zb"] = stack.enter_context(nc.semaphore("sem_zb"))
        sems["w"] = stack.enter_context(nc.semaphore("sem_w"))
        cc_sem = stack.enter_context(nc.semaphore("sem_c"))
        sems["c"] = cc_sem
        cc2_sem = stack.enter_context(nc.semaphore("sem_c2"))
        sems["c2"] = cc2_sem
        sems["z"] = stack.enter_context(nc.semaphore("sem_z"))
        block = stack.enter_context(nc.Block())
        regs = {}

        def getreg(e, name):
            if name not in regs:
                regs[name] = e.alloc_register(name)
            return regs[name]

        for n in CONST_NAMES + CONST_NAMES_U8 + CONST_NAMES_16:
            tgt = {"c_eye": B.eye, "c_ones": B.ones,
                   "c_zeros": B.zeros,
                   "c_masklo8": B.masklo8, "c_maskup8": B.maskup8,
                   "c_maskup08": B.maskup08, "c_eye16": B.eye16,
                   "c_ones16": B.ones16, "c_up016": B.up016,
                   "c_e12716": B.e12716}[n]
            P.op("y", lambda e, b=tgt, a=cins[n]: e.dma_start(
                out=b[:], in_=a[:]), writes=[tgt])
        projdefs = [(wk, 0, 0), (wv, 0, 2048), (wq, 0, 4096),
                    (waf, 0, 6144), (was, 1, 0), (irw1, 1, 2048)]
        for wsrc, bsel, col in projdefs:
            buf = B.big0a if bsel == 0 else B.big0b
            P.op("g", lambda e, s=wsrc, c=col, b=buf: e.dma_start(
                out=b.ap().rearrange(
                    "p (c d) -> p c d", d=128)[:, c // 128:c // 128 + 16, :],
                in_=s.rearrange("(c p) d -> p c d", p=128)),
                writes=[buf])
        P.op("g", lambda e: e.dma_start(
            out=B.ppw16.ap().rearrange("p (c d) -> p c d", d=3),
            in_=pww.rearrange("(c p) d -> p c d", p=128)), writes=[B.ppw16])
        P.op("g", lambda e: e.dma_start(out=B.nm1_16[:], in_=nmw1[:]),
             writes=[B.nm1_16])
        P.op("g", lambda e: e.dma_start(
            out=B.nm2_16.ap().rearrange("p (c d) -> p c d", d=128),
            in_=nmw2.rearrange("(c p) d -> p c d", p=128)), writes=[B.nm2_16])
        P.op("g", lambda e: e.dma_start(out=B.wo16[:], in_=wo[:]),
             writes=[B.wo16])
        P.op("g", lambda e: e.dma_start(out=B.irw2_16[:], in_=irw2[:]),
             writes=[B.irw2_16])
        P.op("g", lambda e: e.dma_start(out=B.nmb2r16[0:1, :],
                                        in_=nmb2.rearrange("(o p) -> o p", o=1)),
             writes=[B.nmb2r16])
        P.op("g", lambda e: e.dma_start(out=B.nw16r[0:1, :], in_=nrm.rearrange("(o p) -> o p", o=1)),
             writes=[B.nw16r])
        P.op("y", lambda e: e.dma_start(out=B.bafc[:], in_=baf.rearrange("(p o) -> p o", o=1)),
             writes=[B.bafc])
        P.op("y", lambda e: e.dma_start(out=B.basc[:], in_=bas.rearrange("(p o) -> p o", o=1)),
             writes=[B.basc])
        P.op("y", lambda e: e.dma_start(out=B.la8f[:], in_=flam.rearrange("(p o) -> p o", o=1)),
             writes=[B.la8f])
        P.op("y", lambda e: e.dma_start(out=B.la8s[:], in_=slam.rearrange("(p o) -> p o", o=1)),
             writes=[B.la8s])
        P.op("y", lambda e: e.dma_start(out=B.irb1c[:], in_=irb1.rearrange("(p o) -> p o", o=1)),
             writes=[B.irb1c])
        P.op("y", lambda e: e.dma_start(out=B.irb2c[:], in_=irb2.rearrange("(o p) -> o p", o=1)),
             writes=[B.irb2c])
        P.op("y", lambda e: e.dma_start(out=B.pwbc[0:3, :], in_=pwb.rearrange("(p o) -> p o", o=1)),
             writes=[B.pwbc])
        P.op("y", lambda e: e.dma_start(
            out=B.nmb1c[:],
            in_=nmb1.rearrange("(c p) -> p c", p=128),
            allow_slow_non_contiguous=True), writes=[B.nmb1c])
        P.op("y", lambda e: e.dma_start(out=B.rsel_sb[:], in_=rsel[:]),
             writes=[B.rsel_sb])
        P.op("y", lambda e: e.dma_start(out=B.rsel2_sb[:], in_=rsel2[:]),
             writes=[B.rsel2_sb])
        P.flush("y")
        P.flush("g")


        for lb in (B.la8f, B.la8s):
            P.op("s", lambda e, b=lb: e.activation(b[:], b[:], AF.Sigmoid),
                 reads=[lb], writes=[lb])
            P.op("s", lambda e, b=lb: e.add(b[:], b[:], f32(1e-8)),
                 reads=[lb], writes=[lb])
            P.op("s", lambda e, b=lb: e.activation(b[:], b[:], AF.Ln),
                 reads=[lb], writes=[lb])
            P.op("s", lambda e, b=lb: e.mul(b[:], b[:], f32(8.0)),
                 reads=[lb], writes=[lb])
        P.op("t", lambda e: e.matmul(
            pb[0][:, 0:128], B.ones16[0:1, :], B.nw16r[0:1, :],
            start=True, stop=True),
            reads=[B.ones16, B.nw16r], writes=[pb[0]])
        P.op("v", lambda e: e.tensor_copy(B.nrmw[:], pb[0][:, 0:128]),
             reads=[pb[0]], writes=[B.nrmw])
        for buf in (B.g0f[0], B.g0f[1], B.g0s[0], B.g0s[1], B.ns0r[0],
                    B.ns0r[1], B.spack, B.stload):
            P.op("v", lambda e, b=buf: e.memset(b[:], 0.0), writes=[buf])
        P.op("v", lambda e: e.tensor_scalar(
            B.rselq_sb[:], B.rsel_sb[:], -1.0, 1.0,
            op0=ALU.mult, op1=ALU.add),
            reads=[B.rsel_sb], writes=[B.rselq_sb])

        for i in range(NL):
            for hf in range(2):
                P.op("y", lambda e, i=i, hf=hf, b=B.xa[hf]: e.dma_start(
                    out=b[:],
                    in_=x[128 * i:128 * (i + 1), 1024 * hf:1024 * (hf + 1)]),
                    writes=[B.xa[hf]])
            P.flush("y")
            for q_ in range(4):
                xb = B.xa[q_ // 2]
                P.op("v", lambda e, b=xb, q_=q_: e.bn_stats(
                    B.lncols[:, 6 * q_:6 * q_ + 6],
                    b[:, 512 * (q_ % 2):512 * (q_ % 2 + 1)]),
                    reads=[xb], writes=[B.lncols])
            P.op("v", lambda e: e.bn_aggr(B.lncols[:, 24:26],
                                          B.lncols[:, 0:24]),
                 reads=[B.lncols], writes=[B.lncols])
            P.op("s", lambda e: e.activation(
                B.lncols[:, 26:27], B.lncols[:, 25:26], AF.Sqrt,
                bias=f32(1e-5)), reads=[B.lncols], writes=[B.lncols])
            P.op("v", lambda e: e.reciprocal(
                B.lncols[:, 27:28], B.lncols[:, 26:27]),
                reads=[B.lncols], writes=[B.lncols])
            for hf in range(2):
                P.op("v", lambda e, hf=hf, b=B.xa[hf]: e.tensor_scalar(
                    B.hid16[:, 4096 + 1024 * hf:4096 + 1024 * (hf + 1)],
                    b[:], B.lncols[:, 24:25],
                    B.lncols[:, 27:28], op0=ALU.subtract, op1=ALU.mult),
                    reads=[B.xa[hf], B.lncols], writes=[B.hid16])
            P.op("y", lambda e, i=i: e.dma_start(
                out=xnsc[128 * i:128 * (i + 1), :],
                in_=B.hid16[:, 4096:6144]),
                reads=[B.hid16], writes=[xnsc])
        P.flush("y")

        P.op("v", lambda e: e.memset(B.pabs[:], 0.0), writes=[B.pabs])
        wtiles = []
        for m, (wsrc, nr, ncol) in enumerate(
                [(w1s, D // 8, H), (w2s, D // 8, H), (w3s, H // 8, D)]):
            tl = []
            r0 = 0
            while r0 < nr:
                p = min(128, nr - r0)
                c0 = 0
                while c0 < ncol:
                    w = min(1024, ncol - c0)
                    tl.append((m, wsrc, r0, p, c0, w))
                    c0 += w
                r0 += p
            wtiles.append(tl)
        for tl in wtiles:
            for k, (m, wsrc, r0, p, c0, w) in enumerate(tl):
                xa = B.xa[k % 2]
                lq = "y" if k % 2 == 0 else "w"
                P.op(lq, lambda e, s=wsrc, r0=r0, p=p, c0=c0, w=w, b=xa:
                     e.dma_start(out=b[0:p, 0:w],
                                 in_=s[r0:r0 + p, c0:c0 + w]), writes=[xa])
                P.flush(lq)
                P.op("v", lambda e, p=p, w=w, b=xa, k=k: e.tensor_reduce(
                    B.lncols[0:p, 10 + (k % 2):11 + (k % 2)], b[0:p, 0:w],
                    axis=AX.X, op=ALU.add,
                    apply_absolute_value=True),
                    reads=[xa], writes=[B.lncols])
                P.op("v", lambda e, m=m, p=p, k=k: e.tensor_tensor(
                    B.pabs[0:p, m:m + 1], B.pabs[0:p, m:m + 1],
                    B.lncols[0:p, 10 + (k % 2):11 + (k % 2)], op=ALU.add),
                    reads=[B.pabs, B.lncols], writes=[B.pabs])
        for m in range(3):
            P.op("t", lambda e, m=m: e.matmul(
                pb[1][0:1, m:m + 1], B.pabs[:, m:m + 1], B.ones[:, 0:1],
                start=True, stop=True),
                reads=[B.pabs, B.ones], writes=[pb[1]])
        P.op("v", lambda e: e.memset(B.grow[:], 0.0), writes=[B.grow])
        P.op("v", lambda e: e.tensor_copy(B.grow[0:1, 0:3], pb[1][0:1, 0:3]),
             reads=[pb[1]], writes=[B.grow])
        P.op("y", lambda e: e.dma_start(out=cc_gin[:], in_=B.grow[:]),
             reads=[B.grow], writes=[cc_gin])
        P.flush("y")
        ccn = [0]

        def collective(kind, op, ins_, outs_, reads=(), after=()):
            ccn[0] += 1
            k = ccn[0]
            P.op("g", lambda e: e.collective_compute(
                kind, op, replica_groups=RG, ins=ins_, outs=outs_
            ).then_inc(cc_sem, 1), reads=list(reads), after=list(after),
                inc=0, register=False)
            return ("c", k)

        def sdma_selfwait(fn, writes):
            tok = P.op("s", fn, writes=list(writes), inc=16, sem_key="z")
            P.op("s", lambda e, t=tok[1]: e.wait_ge(sems["z"], t),
                 inc=0, register=False)
            return tok

        tg = collective("AllReduce", ALU.add, [cc_gin[:]], [cc_gout[:]],
                        reads=[cc_gin])
        P.op("w", lambda e: e.dma_start(out=B.grow2[:], in_=cc_gout[:]),
             writes=[B.grow2], after=[tg])
        P.flush("w")
        NW = float(H * D)
        P.op("s", lambda e: e.mul(B.throw[:], B.grow2[:], f32(0.5 / NW)),
             reads=[B.grow2], writes=[B.throw])
        P.op("s", lambda e: e.add(B.throw[:], B.throw[:], f32(0.5e-8)),
             reads=[B.throw], writes=[B.throw])
        P.op("t", lambda e: e.matmul(
            pb[1][:, 4:7], B.ones[0:1, 0:128], B.throw[0:1, 0:3],
            start=True, stop=True),
            reads=[B.ones, B.throw], writes=[pb[1]])
        P.op("v", lambda e: e.tensor_copy(B.thc[:], pb[1][:, 4:7]),
             reads=[pb[1]], writes=[B.thc])
        P.op("s", lambda e: e.mul(B.nthc[:], B.thc[:], f32(-1.0)),
             reads=[B.thc], writes=[B.nthc])
        ccw = {0: cc_w1in, 1: cc_w2in, 2: cc_w3in}
        qtiles = [t for tl in wtiles for t in tl]

        def emit_qload(k):
            m, wsrc, r0, p, c0, w = qtiles[k]
            xa = B.xa[k % 2]
            lq = "y" if k % 2 == 0 else "w"
            P.op(lq, lambda e, s=wsrc, r0=r0, p=p, c0=c0, w=w, b=xa:
                 e.dma_start(out=b[0:p, 0:w],
                             in_=s[r0:r0 + p, c0:c0 + w]), writes=[xa])
            P.flush(lq)

        def emit_qcomp(k):
            m, wsrc, r0, p, c0, w = qtiles[k]
            xa = B.xa[k % 2]
            qb = B.w3buf[k % 2]
            lq = "y" if k % 2 == 0 else "w"
            P.op("s", lambda e, p=p, w=w, m=m, b=xa, qb=qb: e.activation(
                qb[0:p, 1024:1024 + w], b[0:p, 0:w], AF.Sign,
                bias=B.nthc[0:p, m:m + 1]),
                reads=[xa, B.nthc], writes=[qb])
            P.op("s", lambda e, p=p, w=w, m=m, b=xa, qb=qb: e.activation(
                qb[0:p, 0:w], b[0:p, 0:w], AF.Sign,
                bias=B.thc[0:p, m:m + 1]),
                reads=[xa, B.thc], writes=[qb])
            P.op("v", lambda e, p=p, w=w, qb=qb: e.tensor_tensor(
                qb[0:p, 0:w], qb[0:p, 1024:1024 + w], qb[0:p, 0:w],
                op=ALU.add), reads=[qb], writes=[qb])
            P.op(lq, lambda e, m=m, r0=r0, p=p, c0=c0, w=w, qb=qb:
                 e.dma_start(out=ccw[m][r0:r0 + p, c0:c0 + w],
                             in_=qb[0:p, 0:w]),
                 reads=[qb], writes=[ccw[m]])
            P.flush(lq)

        qpos = [0]
        qloaded = [0]
        nt1 = len(wtiles[0])
        nt2 = nt1 + len(wtiles[1])
        nt3 = nt2 + len(wtiles[2])

        def emit_qtiles(n):
            for _ in range(n):
                k = qpos[0]
                if k >= len(qtiles):
                    return
                if qloaded[0] == k:
                    emit_qload(k)
                    qloaded[0] += 1
                if qloaded[0] < len(qtiles) and qloaded[0] == k + 1:
                    emit_qload(qloaded[0])
                    qloaded[0] += 1
                emit_qcomp(k)
                qpos[0] += 1
                if qpos[0] == nt1:
                    tg1 = collective("AllGather", ALU.bypass, [cc_w1in[:]],
                                     [cc_w1out[:]], reads=[cc_w1in])
                    P.note_write(cc_w1out, tg1)
                elif qpos[0] == nt2:
                    tg2 = collective("AllGather", ALU.bypass, [cc_w2in[:]],
                                     [cc_w2out[:]], reads=[cc_w2in])
                    P.note_write(cc_w2out, tg2)
                elif qpos[0] == nt3:
                    tg3 = collective("AllGather", ALU.bypass, [cc_w3in[:]],
                                     [cc_w3out[:]], reads=[cc_w3in])
                    P.note_write(cc_w3out, tg3)

        ypack = [0]
        projb = {"k": (B.big0a, 0, pb[0]), "v": (B.big0a, 2048, pb[1]),
                 "q": (B.big0a, 4096, pb[2]), "af": (B.big0a, 6144, pb[3]),
                 "as": (B.big0b, 0, pb[4]), "ir": (B.big0b, 2048, pb[5])}
        xn_tok = [None, None]

        def load_xn_half(half):
            for c in range(NF):
                P.op("y", lambda e, c=c, half=half: e.dma_start(
                    out=B.xn_fm[:, 512 * c:512 * (c + 1)],
                    in_=xnsc[512 * half:512 * (half + 1),
                             128 * c:128 * (c + 1)],
                    transpose=True), writes=[B.xn_fm], reads=[xnsc])
            P.flush("y")

        for half in range(2):
            load_xn_half(half)
            hs = slice(512 * half, 512 * (half + 1))
            for name, (wtb, colb, bank) in projb.items():
                for c in range(NF):
                    P.op("t", lambda e, w=wtb, colb=colb, c=c, bank=bank:
                         e.matmul(
                             bank[:, 0:512],
                             w[:, colb + 128 * c:colb + 128 * (c + 1)],
                             B.xn_fm[:, 512 * c:512 * (c + 1)],
                             start=(c == 0), stop=(c == NF - 1)),
                         reads=[wtb, B.xn_fm], writes=[bank])
                emit_qtiles(2)
            for c in range(NF):
                P.op("t", lambda e, c=c: e.matmul(
                    pb[6][0:3, 0:512], B.ppw16[:, 3 * c:3 * (c + 1)],
                    B.xn_fm[:, 512 * c:512 * (c + 1)],
                    start=(c == 0), stop=(c == NF - 1)),
                    reads=[B.ppw16, B.xn_fm], writes=[pb[6]])
            P.op("v", lambda e, hs=hs: e.tensor_copy(
                B.hid16[:, 6144 + hs.start:6144 + hs.stop],
                projb["v"][2][:, 0:512]),
                reads=[projb["v"][2]], writes=[B.hid16])
            for jj in range(4):
                j = 4 * half + jj
                for c in range(NF):
                    P.op("t", lambda e, jj=jj, c=c: e.matmul(
                        pb[6][:, 128:256],
                        B.xn_fm[:, 512 * c + 128 * jj:512 * c
                                + 128 * (jj + 1)],
                        B.big0a[:, 2048 + 128 * c:2048 + 128 * (c + 1)],
                        start=(c == 0), stop=(c == NF - 1)),
                        reads=[B.xn_fm, B.big0a], writes=[pb[6]])
                P.op("v", lambda e, j=j: e.tensor_copy(
                    B.vc_tm[:, 128 * j:128 * (j + 1)], pb[6][:, 128:256]),
                    reads=[pb[6]], writes=[B.vc_tm])
                emit_qtiles(1)
            for name, dst in (("k", B.khat), ("q", B.qhat)):
                bank = projb[name][2]
                P.op("v", lambda e, hs=hs, d=dst, b=bank: e.tensor_copy(
                    d[:, hs], b[:, 0:512]), reads=[bank], writes=[dst])
                P.op("s", lambda e, hs=hs, d=dst: e.activation(
                    B.prj_sq[:], d[:, hs], AF.Square),
                    reads=[dst], writes=[B.prj_sq])
                P.op("t", lambda e: e.matmul(
                    pb[6][0:1, 0:512], B.ones[:, 0:1], B.prj_sq[:],
                    start=True, stop=True),
                    reads=[B.ones, B.prj_sq], writes=[pb[6]])
                P.op("s", lambda e: e.activation(
                    B.rowp[0:1, 1024:1536], pb[6][0:1, 0:512], AF.Sqrt),
                    reads=[pb[6]], writes=[B.rowp])
                P.op("v", lambda e: e.tensor_single_scalar(
                    B.rowp[0:1, 1024:1536], B.rowp[0:1, 1024:1536], 1e-12, op=ALU.max),
                    reads=[B.rowp], writes=[B.rowp])
                P.op("v", lambda e: e.reciprocal(B.rowp[0:1, 1024:1536], B.rowp[0:1, 1024:1536]),
                     reads=[B.rowp], writes=[B.rowp])
                P.op("v", lambda e: e.tensor_copy(B.rowp16[0:1, 1024:1536],
                                                  B.rowp[0:1, 1024:1536]),
                     reads=[B.rowp], writes=[B.rowp16])
                P.op("t", lambda e: e.matmul(
                    pb[6][:, 0:512], B.ones16[0:1, :], B.rowp16[0:1, 1024:1536],
                    start=True, stop=True),
                    reads=[B.ones16, B.rowp16], writes=[pb[6]])
                P.op("v", lambda e, hs=hs, d=dst: e.tensor_tensor(
                    d[:, hs], d[:, hs], pb[6][:, 0:512], op=ALU.mult),
                    reads=[dst, pb[6]], writes=[dst])
            for name, bcol, lacol, dst in (("af", B.bafc, B.la8f, B.laf),
                                           ("as", B.basc, B.la8s, B.las)):
                bank = projb[name][2]
                P.op("s", lambda e, b=bank, bc=bcol: e.activation(
                    B.prj_sq[:], b[:, 0:512], AF.Sigmoid, bias=bc[:]),
                    reads=[bank, bcol], writes=[B.prj_sq])
                P.op("v", lambda e, hs=hs, la=lacol, d=dst:
                     e.tensor_single_scalar(
                         d[:, hs], B.prj_sq[:], la[:, 0:1], op=ALU.mult),
                     reads=[B.prj_sq, lacol], writes=[dst])
            bank = projb["ir"][2]
            P.op("s", lambda e, b=bank: e.activation(
                B.girs16[:], b[:, 0:512], AF.Gelu, bias=B.irb1c[:]),
                reads=[bank, B.irb1c], writes=[B.girs16])
            P.op("t", lambda e: e.matmul(
                pb[6][0:1, 0:512], B.irw2_16[:], B.girs16[:],
                start=True, stop=True),
                reads=[B.irw2_16, B.girs16], writes=[pb[6]])
            P.op("s", lambda e: e.activation(
                B.rowp[0:1, 1536:2048], pb[6][0:1, 0:512], AF.Sigmoid,
                bias=B.irb2c[0:1, 0:1]),
                reads=[pb[6], B.irb2c], writes=[B.rowp])
            P.op("v", lambda e, hs=hs: e.tensor_single_scalar(
                B.gate_row[0:1, hs], B.rowp[0:1, 1536:2048], 0.5, op=ALU.is_gt),
                reads=[B.rowp], writes=[B.gate_row])
            P.op("s", lambda e, hs=hs: e.activation(
                B.pwl16[0:3, hs], pb[6][0:3, 0:512], AF.Identity,
                bias=B.pwbc[0:3, 0:1]),
                reads=[pb[6], B.pwbc], writes=[B.pwl16])
            for hh in range(2):
                P.op("t", lambda e, hh=hh, hs=hs: e.matmul(
                    pb[6][:, 0:512], B.nm1_16[:, 128 * hh:128 * (hh + 1)],
                    B.hid16[:, 6144 + hs.start:6144 + hs.stop],
                    start=True, stop=True),
                    reads=[B.nm1_16, B.hid16], writes=[pb[6]])
                P.op("s", lambda e, hh=hh, half=half: e.activation(
                    B.hid16[:, 1024 * hh + 512 * half:
                            1024 * hh + 512 * half + 512],
                    pb[6][:, 0:512], AF.Gelu, bias=B.nmb1c[:, hh:hh + 1]),
                    reads=[pb[6], B.nmb1c], writes=[B.hid16])
                emit_qtiles(1)
        P.op("v", lambda e: e.tensor_copy(B.gate16[0:1, 0:T],
                                          B.gate_row[0:1, 0:T]),
             reads=[B.gate_row], writes=[B.gate16])
        for j in range(NL):
            half, jj = j // 4, j % 4
            for hh in range(2):
                P.op("t", lambda e, j=j, hh=hh, half=half, jj=jj: e.matmul(
                    pb[6][:, 0:128],
                    B.hid16[:, 1024 * hh + 512 * half + 128 * jj:
                            1024 * hh + 512 * half + 128 * (jj + 1)],
                    B.nm2_16[:, 128 * hh:128 * (hh + 1)],
                    start=(hh == 0), stop=False),
                    reads=[B.hid16, B.nm2_16], writes=[pb[6]])
            P.op("t", lambda e: e.matmul(
                pb[6][:, 0:128], B.ones16[0:1, :], B.nmb2r16[0:1, :],
                start=False, stop=True),
                reads=[B.ones16, B.nmb2r16], writes=[pb[6]])
            P.op("s", lambda e, j=j: e.activation(
                B.zw[:, 128 * j:128 * (j + 1)], pb[6][:, 0:128], AF.Identity,
                scale=f32(0.1)), reads=[pb[6]], writes=[B.zw])

        P.op("v", lambda e: e.memset(B.nsab[:], 0.0), writes=[B.nsab])
        P.op("s", lambda e: e.add(B.nsab[0:1, 0:1], B.nsab[0:1, 0:1],
                                  f32(1.0)),
             reads=[B.nsab], writes=[B.nsab])
        x1psum = [pb[0], pb[1]]
        x2psum = [pb[2], pb[3]]
        units = []
        latq = []
        nlat = [0]

        def push(u):
            units.append(u)
            nlat[0] += 1
            if nlat[0] % 2 == 0 and latq:
                units.append(latq.pop(0))

        for half in range(2):
            if half == 0:
                units.append(lambda: load_xn_half(0))
            for mat, ccout in ((0, cc_w1out), (1, cc_w2out)):
                for win in range(NWIN):
                    wtb = B.big0a if win % 2 == 0 else B.big0b
                    wq = "y" if win % 2 == 0 else "w"

                    def win_load(wtb=wtb, wq=wq, win=win, ccout=ccout):
                        for c in range(NF):
                            P.op(wq, lambda e, c=c, win=win, w=wtb,
                                 co=ccout: e.dma_start(
                                     out=w[:, 512 * c:512 * (c + 1)],
                                     in_=co[128 * c:128 * (c + 1),
                                            512 * win:512 * (win + 1)]),
                                 writes=[wtb], reads=[ccout])
                        P.flush(wq)
                    push(win_load)
                    for hcin in range(HWIN):
                        hc = HWIN * win + hcin
                        bank = (x1psum if mat == 0 else x2psum)[hc % 2]

                        def hcin_unit(wtb=wtb, bank=bank, hc=hc,
                                      hcin=hcin, mat=mat):
                            for c in range(NF):
                                P.op("t", lambda e, c=c, hcin=hcin,
                                     w=wtb, bank=bank: e.matmul(
                                         bank[:, 0:512],
                                         w[:, 512 * c + 128 * hcin:
                                           512 * c + 128 * (hcin + 1)],
                                         B.xn_fm[:, 512 * c:512 * (c + 1)],
                                         start=(c == 0), stop=(c == NF - 1)),
                                     reads=[wtb, B.xn_fm], writes=[bank],
                                     quiet=(c != NF - 1))
                            if mat == 0:
                                P.op("s", lambda e, hc=hc, bank=bank:
                                     e.activation(
                                         B.hid16[:, 512 * hc:
                                                 512 * (hc + 1)],
                                         bank[:, 0:512], AF.Identity),
                                     reads=[bank], writes=[B.hid16])
                            else:
                                P.op("s", lambda e, hc=hc: e.activation(
                                    B.girs16[:],
                                    B.hid16[:, 512 * hc:512 * (hc + 1)],
                                    AF.Sigmoid, scale=f32(0.5)),
                                    reads=[B.hid16], writes=[B.girs16])
                                P.op("v", lambda e, hc=hc, bank=bank:
                                     e.tensor_tensor(
                                         B.hid16[:, 512 * hc:
                                                 512 * (hc + 1)],
                                         B.girs16[:], bank[:, 0:512],
                                         op=ALU.mult),
                                     reads=[B.girs16, bank],
                                     writes=[B.hid16])
                        push(hcin_unit)
            if half == 0:
                units.append(lambda: load_xn_half(1))
            for db in range(4):
                ds = slice(512 * db, 512 * (db + 1))
                obank = {0: pb[0], 1: pb[1], 2: pb[2], 3: pb[3]}
                for grp in range(NH // W3G):
                    wb = B.w3buf[grp % 2]
                    wq = "y" if grp % 2 == 0 else "w"

                    def w3_unit(db=db, grp=grp, wb=wb, wq=wq,
                                obank=obank, half=half, ds=ds):
                        for gi in range(W3G):
                            hc = W3G * grp + gi
                            P.op(wq, lambda e, db=db, hc=hc, gi=gi, wb=wb:
                                 e.dma_start(
                                     out=wb[:, 512 * gi:512 * (gi + 1)],
                                     in_=cc_w3out[128 * hc:128 * (hc + 1),
                                                  512 * db:512 * (db + 1)]),
                                 writes=[wb], reads=[cc_w3out])
                        P.flush(wq)
                        if grp in (7, 8, 9):
                            ts_ = grp - 7
                            xs = (B.xa[0], B.prj_sq, B.stload)[ts_]
                            j = 4 * half + ts_
                            P.op("y", lambda e, j=j, ds=ds, xs=xs:
                                 e.dma_start(
                                     out=xs[:, 0:512],
                                     in_=x[128 * j:128 * (j + 1), ds]),
                                 writes=[xs])
                            P.flush("y")
                        for gi in range(W3G):
                            hc = W3G * grp + gi
                            last = (hc == NH - 1)
                            for ts_ in range(4):
                                P.op("t", lambda e, hc=hc, gi=gi, ts_=ts_,
                                     wb=wb, last=last, b=obank[ts_]:
                                     e.matmul(
                                         b[:, 0:512],
                                         B.hid16[:, 512 * hc + 128 * ts_:
                                                 512 * hc
                                                 + 128 * (ts_ + 1)],
                                         wb[:, 512 * gi:512 * (gi + 1)],
                                         start=(hc == 0), stop=last),
                                     reads=[B.hid16, wb],
                                     writes=[obank[ts_]],
                                     quiet=(not last and not (
                                         gi == W3G - 1 and ts_ == 3)))
                    push(w3_unit)
                for ts_ in range(4):
                    def strip_unit(half=half, ts_=ts_, ds=ds, obank=obank):
                        j = 4 * half + ts_
                        if ts_ < 3:
                            xs = (B.xa[0], B.prj_sq, B.stload)[ts_]
                        else:
                            xs = B.xa[0]
                            P.op("y", lambda e, j=j, ds=ds, xs=xs:
                                 e.dma_start(
                                     out=xs[:, 0:512],
                                     in_=x[128 * j:128 * (j + 1), ds]),
                                 writes=[xs])
                            P.flush("y")
                        P.op("v", lambda e, xs=xs, b=obank[ts_]:
                             e.scalar_tensor_tensor(
                                 xs[:, 0:512], b[:, 0:512], 0.25,
                                 xs[:, 0:512], op0=ALU.mult, op1=ALU.add),
                             reads=[xs, obank[ts_]], writes=[xs])
                        P.op("y", lambda e, j=j, ds=ds, xs=xs: e.dma_start(
                            out=y_out[128 * j:128 * (j + 1), ds],
                            in_=xs[:, 0:512]),
                            reads=[xs], writes=[y_out])
                    units.append(strip_unit)
                for jj in range(4):
                    def epi_unit(half=half, jj=jj, db=db):
                        j = 4 * half + jj
                        eds = slice(512 * db, 512 * (db + 1))
                        bank = pb[5]
                        xs = B.xa[1]
                        P.op("t", lambda e, j=j, eds=eds, bank=bank:
                             e.matmul(
                                 bank[:, 0:512],
                                 B.o_vm[:, 128 * j:128 * (j + 1)],
                                 B.wo16[:, eds], start=True, stop=True),
                             reads=[B.o_vm, B.wo16], writes=[bank])
                        P.op("y", lambda e, j=j, eds=eds, xs=xs:
                             e.dma_start(
                                 out=xs[:, 0:512],
                                 in_=y_out[128 * j:128 * (j + 1), eds]),
                             writes=[xs], reads=[y_out])
                        P.flush("y")
                        P.op("v", lambda e, xs=xs, bank=bank:
                             e.tensor_tensor(xs[:, 0:512], xs[:, 0:512],
                                             bank[:, 0:512], op=ALU.add),
                             reads=[xs, bank], writes=[xs])
                        P.op("y", lambda e, j=j, eds=eds, xs=xs:
                             e.dma_start(
                                 out=y_out[128 * j:128 * (j + 1), eds],
                                 in_=xs[:, 0:512]),
                             reads=[xs], writes=[y_out])
                    latq.append(epi_unit)
        ucur = [0]

        def emit_unit(n=1):
            for _ in range(n):
                if ucur[0] < len(units):
                    units[ucur[0]]()
                    ucur[0] += 1

        class FillP:
            def __init__(self, P):
                self._P = P

            def op(self, q, fn, **kw):
                tok = self._P.op(q, fn, **kw)
                if q == "t":
                    emit_unit(1)
                return tok

            def __getattr__(self, n):
                return getattr(self._P, n)

        units.extend(latq)
        del latq[:]

        PF = FillP(P)

        psum_lite = dict(psum)
        psum_lite["pO"] = pb[2]
        emit_qtiles(len(qtiles) - qpos[0])
        for j in range(NL):
            emit_prep_chunk(P, B, psum, j, [])
            if j > 0:
                emit_relay_lite(P, B, psum_lite, j - 1)
                emit_tchain(P, B, pb, j - 1)
                emit_unit(2)
            if j > 1:
                emit_unit(4)
        emit_relay_lite(P, B, psum_lite, NL - 1)
        emit_tchain(P, B, pb, NL - 1)
        fin = NL % 2
        P.op("v", lambda e: e.tensor_copy(B.spack[:, 0:128], B.g0f[fin][:]),
             reads=[B.g0f[fin]], writes=[B.spack])
        P.op("v", lambda e: e.tensor_copy(B.spack[:, 128:256],
                                          B.g0s[fin][:]),
             reads=[B.g0s[fin]], writes=[B.spack])
        P.op("t", lambda e: e.transpose(pb[1][:, 128:256],
                                        B.prj_sq[:, 128:256], B.eye[:]),
             reads=[B.prj_sq, B.eye], writes=[pb[1]])
        P.op("t", lambda e: e.transpose(pb[1][:, 256:384],
                                        B.prj_sq[:, 256:384], B.eye[:]),
             reads=[B.prj_sq, B.eye], writes=[pb[1]])
        P.op("v", lambda e: e.tensor_copy(B.prj_sq[:, 128:384],
                                          pb[1][:, 128:384]),
             reads=[pb[1]], writes=[B.prj_sq])
        P.op("s", lambda e: e.nop(), reads=[B.spack, B.prj_sq], inc=0)
        sdma_selfwait(lambda e: e.dma_start(
            out=cc_rin[:, 0:256], in_=B.spack[:]), [cc_rin])
        sdma_selfwait(lambda e: e.dma_start(
            out=cc_rin[:, 256:512], in_=B.prj_sq[:, 128:384]), [cc_rin])
        P.op("s", lambda e: e.nop(), reads=[B.nsab], inc=0)
        sdma_selfwait(lambda e: e.dma_start(
            out=cc_nsin[:], in_=B.nsab[:]), [cc_nsin])

        P.op("g", lambda e: e.collective_compute(
            "AllGather", ALU.bypass, replica_groups=RG,
            ins=[cc_rin[:]], outs=[cc_rout[:]]).then_inc(cc2_sem, 1),
            reads=[cc_rin], inc=0, register=False)
        P.op("g", lambda e: e.collective_compute(
            "AllGather", ALU.bypass, replica_groups=RG,
            ins=[cc_nsin[:]], outs=[cc_nsout[:]]).then_inc(cc2_sem, 1),
            reads=[cc_nsin], inc=0, register=False)

        emit_unit(12)
        for j in range(NL):
            P.op("t", lambda e, j=j: e.matmul(
                pb[6][:, 0:3], B.pwl16[0:3, 128 * j:128 * (j + 1)],
                B.eye16[0:3, 0:3], start=True, stop=True),
                reads=[B.pwl16, B.eye16], writes=[pb[6]])
            P.op("v", lambda e: e.reduce_max(
                B.pwsc[:, 0:1], pb[6][:, 0:3], axis=AX.X),
                reads=[pb[6]], writes=[B.pwsc])
            P.op("v", lambda e: e.tensor_single_scalar(
                B.pwtmp[:], pb[6][:, 0:3], B.pwsc[:, 0:1], op=ALU.subtract),
                reads=[pb[6], B.pwsc], writes=[B.pwtmp])
            P.op("s", lambda e: e.activation(B.pwtmp[:], B.pwtmp[:], AF.Exp),
                 reads=[B.pwtmp], writes=[B.pwtmp])
            P.op("v", lambda e: e.tensor_reduce(
                B.pwsc[:, 1:2], B.pwtmp[:], axis=AX.X, op=ALU.add),
                reads=[B.pwtmp], writes=[B.pwsc])
            P.op("v", lambda e: e.reciprocal(B.pwsc[:, 1:2], B.pwsc[:, 1:2]),
                 reads=[B.pwsc], writes=[B.pwsc])
            P.op("v", lambda e, j=j: e.tensor_single_scalar(
                B.pw[:, 3 * j:3 * j + 3], B.pwtmp[:], B.pwsc[:, 1:2],
                op=ALU.mult), reads=[B.pwtmp, B.pwsc], writes=[B.pw])
        P.op("s", lambda e: e.wait_ge(cc2_sem, 2), inc=0, register=False)
        sdma_selfwait(lambda e: e.dma_start(
            out=B.rowp[0:1, 0:1056].rearrange("o (s c) -> o s c", c=132),
            in_=cc_nsout.rearrange("(o s) c -> o s c", o=1)), [B.rowp])
        P.op("v", lambda e: e.memset(B.nsab[0:1, 4:132], 0.0),
             writes=[B.nsab])
        P.op("v", lambda e: e.memset(B.rowp[0:1, 1056:1184], 0.0),
             reads=[B.rowp], writes=[B.rowp])
        for i_ in range(7):
            P.op("v", lambda e, i_=i_: e.tensor_single_scalar(
                B.nsab[0:1, 4:132], B.nsab[0:1, 4:132],
                B.rowp[0:1, 132 * i_:132 * i_ + 1], op=ALU.mult),
                reads=[B.nsab, B.rowp], writes=[B.nsab])
            P.op("v", lambda e, i_=i_: e.tensor_tensor(
                B.nsab[0:1, 4:132], B.nsab[0:1, 4:132],
                B.rowp[0:1, 132 * i_ + 4:132 * i_ + 132], op=ALU.add),
                reads=[B.nsab, B.rowp], writes=[B.nsab])
            P.op("v", lambda e, i_=i_: e.scalar_tensor_tensor(
                B.rowp[0:1, 1056:1184], B.nsab[0:1, 4:132],
                B.rsel2_sb[0:1, i_:i_ + 1], B.rowp[0:1, 1056:1184],
                op0=ALU.mult, op1=ALU.add),
                reads=[B.nsab, B.rsel2_sb, B.rowp], writes=[B.rowp])
        P.op("v", lambda e: e.tensor_copy(B.nsab[0:1, 4:132],
                                          B.rowp[0:1, 1056:1184]),
             reads=[B.rowp], writes=[B.nsab])
        P.op("s", lambda e: e.wait_ge(cc2_sem, 1), inc=0, register=False)
        P.op("v", lambda e: e.memset(B.stinit[:], 0.0), writes=[B.stinit])
        P.op("v", lambda e: e.memset(B.st16f[:], 0.0), writes=[B.st16f])
        P.op("v", lambda e: e.memset(B.st16s[:], 0.0), writes=[B.st16s])
        pR = psum["pR"]
        for hj in range(7):
            sdma_selfwait(lambda e, hj=hj: e.dma_start(
                out=B.stload[:], in_=cc_rout[128 * hj:128 * (hj + 1), :]),
                [B.stload])
            for s_, st16, po in ((0, B.st16f, 0), (1, B.st16s, 128)):
                P.op("v", lambda e, s_=s_: e.tensor_copy(
                    B.tch16[:], B.stload[:, 256 + 128 * s_:384 + 128 * s_]),
                    reads=[B.stload], writes=[B.tch16])
                P.op("t", lambda e, st16=st16, po=po: e.matmul(
                    pR[:, po:po + 128], B.tch16[:], st16[:],
                    start=True, stop=True),
                    reads=[B.tch16, st16], writes=[pR])
                P.op("v", lambda e, s_=s_, hj=hj: e.tensor_single_scalar(
                    B.prj_sq[:, 0:128], B.stinit[:, 128 * s_:128 * (s_ + 1)],
                    B.rselq_sb[:, hj:hj + 1], op=ALU.mult),
                    reads=[B.stinit, B.rselq_sb], writes=[B.prj_sq])
                P.op("v", lambda e, hj=hj, po=po: e.scalar_tensor_tensor(
                    B.prj_sq[:, 0:128], pR[:, po:po + 128],
                    B.rsel_sb[:, hj:hj + 1],
                    B.prj_sq[:, 0:128], op0=ALU.mult, op1=ALU.add),
                    reads=[pR, B.rsel_sb, B.prj_sq], writes=[B.prj_sq])
                P.op("v", lambda e, s_=s_, hj=hj: e.scalar_tensor_tensor(
                    B.stinit[:, 128 * s_:128 * (s_ + 1)],
                    B.stload[:, 128 * s_:128 * (s_ + 1)],
                    B.rsel_sb[:, hj:hj + 1], B.prj_sq[:, 0:128],
                    op0=ALU.mult, op1=ALU.add),
                    reads=[B.stload, B.rsel_sb, B.prj_sq], writes=[B.stinit])
                P.op("v", lambda e, s_=s_, st16=st16: e.tensor_copy(
                    st16[:], B.stinit[:, 128 * s_:128 * (s_ + 1)]),
                    reads=[B.stinit], writes=[st16])
        tr1 = P.op("v", lambda e: e.tensor_copy(B.g0f[0][:],
                                                B.stinit[:, 0:128]),
                   reads=[B.stinit], writes=[B.g0f[0]])
        tr2 = P.op("v", lambda e: e.tensor_copy(B.g0s[0][:],
                                                B.stinit[:, 128:256]),
                   reads=[B.stinit], writes=[B.g0s[0]])
        for r in range(NL):
            emit_replay_chunk(PF, B, psum, r,
                              state_deps=[tr1, tr2] if r == 0 else ())
            while ucur[0] < (r + 1) * 26:
                emit_unit(1)
        emit_unit(len(units) - ucur[0])

        P.flush("y")
        P.flush("g")
        P.flush("w")
        fins = [("ya", P.dma_base["ya"]), ("yb", P.dma_base["yb"]),
                ("za", P.dma_base["za"]), ("zb", P.dma_base["zb"])]
        P.op("y", lambda e: e.nop(), after=fins, inc=0,
             register=False)
        P.emit(block, sems)
    return nc, ypack[0]


def _get_nc():
    if "nc" not in _CACHE:
        _CACHE["nc"], _CACHE["ypack"] = build()
    return _CACHE["nc"], _CACHE["ypack"]


def kernel(**inputs):
    nc, ypack = _get_nc()
    consts = np_consts()
    xfull = np.asarray(inputs["L0_attn_x"], np.float32).reshape(8192, 2048)
    base = {
        "wk": inputs["L0_attn_Wk"], "wv": inputs["L0_attn_Wv"],
        "wq": inputs["L0_attn_Wq"], "waf": inputs["L0_attn_Waf"],
        "was": inputs["L0_attn_Was"], "baf": inputs["L0_attn_baf"],
        "bas": inputs["L0_attn_bas"], "flam": inputs["L0_attn_fast_lambda"],
        "slam": inputs["L0_attn_slow_lambda"],
        "nmw1": inputs["L0_attn_nm_w1"], "nmb1": inputs["L0_attn_nm_b1"],
        "nmw2": inputs["L0_attn_nm_w2"], "nmb2": inputs["L0_attn_nm_b2"],
        "irw1": inputs["L0_attn_ir_w1"], "irb1": inputs["L0_attn_ir_b1"],
        "irw2": inputs["L0_attn_ir_w2"], "irb2": inputs["L0_attn_ir_b2"],
        "pww": inputs["L0_attn_pw_w"], "pwb": inputs["L0_attn_pw_b"],
        "wo": inputs["L0_attn_Wo"], "nrm": inputs["L0_attn_norm_w"],
    }
    base = {k: np.ascontiguousarray(np.asarray(v, np.float32))
            for k, v in base.items()}
    for n in CONST_NAMES + CONST_NAMES_U8 + CONST_NAMES_16:
        base[n] = consts[n]
    w1 = np.asarray(inputs["L0_ffn_w1"], np.float32)
    w2 = np.asarray(inputs["L0_ffn_w2"], np.float32)
    w3 = np.asarray(inputs["L0_ffn_w3"], np.float32)
    in_maps = []
    for c in range(N_CORES):
        m = dict(base)
        m["x"] = np.ascontiguousarray(xfull[1024 * c:1024 * (c + 1)])
        m["w1s"] = np.ascontiguousarray(w1[:, 256 * c:256 * (c + 1)].T)
        m["w2s"] = np.ascontiguousarray(w2[:, 256 * c:256 * (c + 1)].T)
        m["w3s"] = np.ascontiguousarray(w3[:, 704 * c:704 * (c + 1)].T)
        rsel = np.zeros((128, 8), np.float32)
        rsel[:, :c] = 1.0
        m["rsel"] = rsel
        rsel2 = np.zeros((1, 8), np.float32)
        if c > 0:
            rsel2[0, c - 1] = 1.0
        m["rsel2"] = rsel2
        in_maps.append(m)
    res = run_bass_kernel_spmd(nc, in_maps, core_ids=list(range(N_CORES)))
    y = np.concatenate([r["y"] for r in res.results], axis=0)
    return y.reshape(1, 8192, 2048).astype(np.float32)


# version 30
# speedup vs baseline: 1.0586x; 1.0586x over previous
import numpy as np
import ml_dtypes
import concourse.bass as bass
import concourse.mybir as mybir
from concourse.bass_utils import run_bass_kernel_spmd
from contextlib import ExitStack

F32 = mybir.dt.float32
BF16 = mybir.dt.bfloat16
I32 = mybir.dt.int32
U8 = mybir.dt.uint8
AF = mybir.ActivationFunctionType
ALU = mybir.AluOpType
AX = mybir.AxisListType
f32 = np.float32

N_CORES = 8
NL = 8
T = NL * 128
NF = 16
NH = 44
HWIN = 4
NWIN = 11
W3G = 4
D = 2048
H = 5632
K_HORNER_F = 12
K_HORNER_S = 6
RG = [list(range(N_CORES))]


class Prog:
    ENGINES = ("t", "v", "s", "y", "g")
    DMA_ENGINES = ("y", "g")

    QPAR = {"y": ("ya", "yb"), "w": ("za", "zb")}

    def __init__(self):
        self.items = []
        self.cnt = {k: 0 for k in self.ENGINES}
        self.cnt["w"] = 0
        self.cnt["z"] = 0
        for q, (a, b) in self.QPAR.items():
            self.cnt[a] = 0
            self.cnt[b] = 0
        self.res = {}
        self.excl_ids = set()
        self.dma_base = {k: 0 for k in ("y", "g", "ya", "yb", "za", "zb")}
        self.par = {"y": "ya", "w": "za"}
        self.dma_pend = {k: {"w": [], "r": []} for k in ("y", "g", "w")}

    def mark_excl(self, h):
        self.excl_ids.add(id(h))

    def _res(self, h):
        key = id(h)
        if key not in self.res:
            self.res[key] = {"w": [], "r": [], "excl": key in self.excl_ids,
                             "pend_q": None, "pend_w": False}
        return self.res[key]

    def flush(self, q):
        sq = self.par[q] if q in self.QPAR else q
        if self.cnt[sq] == self.dma_base[sq]:
            return
        self.dma_base[sq] = self.cnt[sq]
        tok = (sq, self.dma_base[sq])
        if q in self.QPAR:
            a, b = self.QPAR[q]
            self.par[q] = b if self.par[q] == a else a
        for h in self.dma_pend[q]["w"]:
            r = self._res(h)
            r["w"].append(tok)
            r["r"] = []
            r["pend_q"] = None
            r["pend_w"] = False
        for h in self.dma_pend[q]["r"]:
            r = self._res(h)
            r["r"].append(tok)
            r["pend_q"] = None
        self.dma_pend[q] = {"w": [], "r": []}
        return tok

    def _settle(self, h):
        r = self._res(h)
        if r["pend_q"] is not None:
            self.flush(r["pend_q"])

    def note_write(self, h, tok):
        self._settle(h)
        r = self._res(h)
        r["w"] = [tok]
        r["r"] = []

    def op(self, eng, fn, reads=(), writes=(), after=(), inc=None,
           register=True, sem_key=None, quiet=False):
        is_dma = eng in self.DMA_ENGINES or eng == "w"
        if inc is None:
            inc = 16 if is_dma else 1
        if quiet:
            inc = 0
        waits = list(t for t in after if t is not None)
        for h in reads:
            r = self._res(h)
            if r["pend_w"] or (r["excl"] and r["pend_q"] is not None):
                self.flush(r["pend_q"])
        for h in writes:
            r = self._res(h)
            if r["pend_q"] is not None:
                if is_dma and r["pend_q"] == eng and r["pend_w"]:
                    continue
                self.flush(r["pend_q"])
        skey = sem_key or (self.par[eng] if eng in self.QPAR else eng)
        if is_dma and self.dma_base[skey] > 0:
            waits.append((skey, self.dma_base[skey]))
        assert not (is_dma and sem_key), "sem_key dma uses own discipline"
        for h in reads:
            r = self._res(h)
            waits.extend(r["w"])
            if r["excl"]:
                waits.extend(r["r"])
        for h in writes:
            r = self._res(h)
            waits.extend(r["w"])
            waits.extend(r["r"])
        tok = (skey, self.cnt[skey] + inc)
        if register:
            for h in reads:
                r = self._res(h)
                if is_dma:
                    self.dma_pend[eng]["r"].append(h)
                    r["pend_q"] = eng
                elif r["excl"]:
                    r["w"] = [tok]
                    r["r"] = []
                else:
                    r["r"].append(tok)
            for h in writes:
                r = self._res(h)
                if is_dma:
                    self.dma_pend[eng]["w"].append(h)
                    r["pend_q"] = eng
                    r["pend_w"] = True
                else:
                    r["w"] = [tok]
                    r["r"] = []
        best = {}
        for sk, sv in waits:
            best[sk] = max(best.get(sk, 0), sv)
        if skey in best:
            assert best[skey] <= self.cnt[skey], (skey, best[skey],
                                                  self.cnt[skey])
        self.items.append((eng, sorted(best.items()), fn, inc, skey))
        self.cnt[skey] += inc
        if inc == 0 and not quiet:
            return None
        return tok

    def emit(self, block, sems):
        names = {"t": "tensor", "v": "vector", "s": "scalar",
                 "y": "sync", "g": "gpsimd", "w": "scalar"}
        blockkeys = {}
        for it in self.items:
            blockkeys.setdefault(names[it[0]], []).append(it)
        for bname, items in blockkeys.items():
            def stream(e, items=items):
                seen = {k: 0 for k in sems}
                for _, waits, fn, inc, skey in items:
                    for sk, sv in waits:
                        if sv > seen[sk]:
                            e.wait_ge(sems[sk], sv)
                            seen[sk] = sv
                    inst = fn(e)
                    if inst is not None and inc > 0:
                        inst.then_inc(sems[skey], inc)

            getattr(block, bname)(stream)


def np_consts():
    i = np.arange(128)
    p, fr = i[:, None], i[None, :]
    e127 = np.zeros((128, 1), np.float32)
    e127[127, 0] = 1.0
    return {
        "c_masklo": (fr < p).astype(np.float32),
        "c_maskup": (fr > p).astype(np.float32),
        "c_maskup0": (fr >= p).astype(np.float32),
        "c_eye": np.eye(128, dtype=np.float32),
        "c_ones": np.ones((128, 128), dtype=np.float32),
        "c_zeros": np.zeros((128, 128), dtype=np.float32),
        "c_e127": e127,
        "c_masklo8": (fr < p).astype(np.uint8),
        "c_eye16": np.eye(128, dtype=ml_dtypes.bfloat16),
        "c_ones16": np.ones((128, 128), dtype=ml_dtypes.bfloat16),
        "c_up016": (fr >= p).astype(ml_dtypes.bfloat16),
        "c_e12716": e127.astype(ml_dtypes.bfloat16),
        "c_maskup8": (fr > p).astype(np.uint8),
        "c_maskup08": (fr >= p).astype(np.uint8),
    }


CONST_NAMES = ["c_eye", "c_ones", "c_zeros"]
CONST_NAMES_U8 = ["c_masklo8", "c_maskup8", "c_maskup08"]
CONST_NAMES_16 = ["c_eye16", "c_ones16", "c_up016", "c_e12716"]


class Bufs:
    def __init__(self, nc, stack):
        def sb(name, shape, dt=F32):
            return stack.enter_context(nc.sbuf_tensor(name, shape, dt))

        self.khat = sb("sc_khat", [128, T], BF16)
        self.qhat = sb("sc_qhat", [128, T], BF16)
        self.laf = sb("sc_laf", [128, T])
        self.las = sb("sc_las", [128, T])
        self.rowp = sb("sc_rowp", [1, 2048])
        self.rowp16 = sb("sc_rowp16", [1, 1536], BF16)
        self.gate_row = self.rowp
        self.gate16 = self.rowp16
        self.zw = sb("sc_zw", [128, T], BF16)
        self.pw = sb("sc_pw", [128, 3 * NL])
        self.vc_tm = sb("sc_vctm", [128, T], BF16)
        self.eye = sb("sbc_eye", [128, 128])
        self.ones = sb("sbc_ones", [128, 128])
        self.zeros = sb("sbc_zeros", [128, 128])
        self.eye16 = sb("sbc_eye16", [128, 128], BF16)
        self.ones16 = sb("sbc_ones16", [128, 128], BF16)
        self.up016 = sb("sbc_up016", [128, 128], BF16)
        self.e12716 = sb("sbc_e12716", [128, 1], BF16)
        self.masklo8 = sb("sbc_masklo8", [128, 128], U8)
        self.maskup8 = sb("sbc_maskup8", [128, 128], U8)
        self.maskup08 = sb("sbc_maskup08", [128, 128], U8)
        for st in ("f", "s"):
            for nm in ("kt", "qth", "qtp", "ct2", "tt", "aqc", "tat", "aqp"):
                setattr(self, f"{nm}_{st}", sb(f"sc_{nm}{st}", [128, T], BF16))
        self.dend_f = sb("sc_dendf", [128, NL])
        self.dend_s = sb("sc_dends", [128, NL])
        self.eh_f = sb("sc_ehf", [128, NL])
        self.eh_s = sb("sc_ehs", [128, NL])
        self.gcol = sb("sc_gcol", [128, NL])
        for st_ in ("f", "s"):
            setattr(self, f"lam_{st_}", sb(f"sc_lam_{st_}", [128, 128]))
            setattr(self, f"t1_{st_}", sb(f"sc_t1_{st_}", [128, 128]))
            setattr(self, f"t2_{st_}", sb(f"sc_t2_{st_}", [128, 128]))
            setattr(self, f"hcol_{st_}", sb(f"sc_hcol_{st_}", [128, 4]))
        for p_ in (0, 1):
            for st_ in ("f", "s"):
                setattr(self, f"ct16_{p_}{st_}",
                        sb(f"sc_ct16_{p_}{st_}", [128, 128], BF16))
                setattr(self, f"lraw{p_}{st_}",
                        sb(f"sc_lraw{p_}{st_}", [128, 128], BF16))
                setattr(self, f"zh0_{p_}{st_}",
                        sb(f"sc_zh0_{p_}{st_}", [128, 128], BF16))
                setattr(self, f"zh1_{p_}{st_}",
                        sb(f"sc_zh1_{p_}{st_}", [128, 128], BF16))
            setattr(self, f"w1_{p_}", sb(f"sc_w1_{p_}", [128, 128]))
            setattr(self, f"lgat{p_}", sb(f"sc_lgat{p_}", [128, 128], BF16))
        self.g0f = [sb("sc_g0f0", [128, 128]), sb("sc_g0f1", [128, 128])]
        self.g0s = [sb("sc_g0s0", [128, 128]), sb("sc_g0s1", [128, 128])]
        self.g0h16f = sb("sc_g0hf", [128, 128], BF16)
        self.g0h16s = sb("sc_g0hs", [128, 128], BF16)
        self.bv16f = sb("sc_bvf", [128, 128], BF16)
        self.bv16s = sb("sc_bvs", [128, 128], BF16)
        self.m0w = sb("sc_m0w", [128, 256], BF16)
        self.dm16 = sb("sc_dm16", [128, 128], BF16)
        self.m1f = sb("sc_m1f", [128, 128], BF16)
        self.m1s = sb("sc_m1s", [128, 128], BF16)
        self.yf = sb("sc_yf", [128, 128], BF16)
        self.ys = sb("sc_ys", [128, 128], BF16)
        self.dbuf = sb("sc_dbuf", [128, 128])
        self.sqs = self.dbuf
        self.scol = sb("sc_scol", [128, 8])
        self.pcol = sb("sc_pcol", [128, 2])
        self.pcol16 = sb("sc_pcol16", [128, 1], BF16)
        self.ns0r = [sb("sc_ns0r0", [1, 128], BF16),
                     sb("sc_ns0r1", [1, 128], BF16)]
        self.pcol8 = sb("sc_pcol8", [128, NL])
        self.ipcol8 = sb("sc_ipcol8", [128, NL])
        self.scol8 = sb("sc_scol8", [128, NL])
        self.ucol8 = sb("sc_ucol8", [128, NL])
        self.stinit = sb("sc_stinit", [128, 256])
        self.nsc = sb("sc_nsc", [128, T], BF16)
        self.out_tm = sb("sc_outtm", [128, T], BF16)
        self.o_vm = sb("sc_ovm", [128, T], BF16)
        self.nrmw = sb("sc_nrmw", [128, 128])
        self.rmsc = sb("sc_rmsc", [128, 2])

        self.big0a = sb("k_big0a", [128, 8192], BF16)
        self.big0b = sb("k_big0b", [128, 8192], BF16)
        self.xn_fm = sb("k_xnfm", [128, NF * 512], BF16)
        self.hid16 = sb("k_hid16", [128, NH * 512], BF16)
        self.xa = [sb("k_xa0", [128, 1024]), sb("k_xa1", [128, 1024])]
        self.wo16 = sb("k_wo16", [128, 2048], BF16)
        self.ppw16 = sb("k_ppw16", [128, 48], BF16)
        self.nm1_16 = sb("k_nm1", [128, 256], BF16)
        self.nm2_16 = sb("k_nm2", [128, 256], BF16)
        self.nmb1c = sb("k_nmb1c", [128, 2])
        self.nmb2r16 = sb("k_nmb2r", [1, 128], BF16)
        self.bafc = sb("k_bafc", [128, 1])
        self.basc = sb("k_basc", [128, 1])
        self.la8f = sb("k_la8f", [128, 1])
        self.la8s = sb("k_la8s", [128, 1])
        self.irb1c = sb("k_irb1c", [128, 1])
        self.irw2_16 = sb("k_irw2", [128, 1], BF16)
        self.irb2c = sb("k_irb2c", [1, 1])
        self.pwbc = sb("k_pwbc", [128, 1])
        self.nw16r = sb("k_nw16r", [1, 128], BF16)
        self.lncols = sb("k_lncols", [128, 32])
        self.pabs = sb("k_pabs", [128, 4])
        self.grow = sb("k_grow", [1, 4])
        self.grow2 = sb("k_grow2", [1, 4])
        self.throw = sb("k_throw", [1, 4])
        self.thc = sb("k_thc", [128, 3])
        self.nthc = sb("k_nthc", [128, 3])
        self.prj_sq = sb("k_prjsq", [128, 512])
        self.girs16 = sb("k_girs16", [128, 512], BF16)
        self.pwl16 = sb("k_pwl16", [3, T], BF16)
        self.nsab = sb("k_nsab", [1, 132])
        self.pwtmp = sb("k_pwtmp", [128, 3])
        self.pwsc = sb("k_pwsc", [128, 2])
        self.w3buf = [sb("k_w3b0", [128, W3G * 512], BF16),
                      sb("k_w3b1", [128, W3G * 512], BF16)]
        self.stload = sb("k_stload", [128, 512])
        self.spack = sb("k_spack", [128, 256])
        self.rsel_sb = sb("k_rsel", [128, 8])
        self.rselq_sb = sb("k_rselq", [128, 8])
        self.rsel2_sb = sb("k_rsel2", [1, 8])
        self.deh16 = self.yf
        self.tch16 = self.dm16
        self.pjt16 = self.ys
        self.st16f = self.m1f
        self.st16s = self.m1s


def emit_prep_chunk(P, B, psum, j, deps):
    cs = slice(j * 128, (j + 1) * 128)
    jc = slice(j, j + 1)
    p_ = j % 2
    w1 = getattr(B, f"w1_{p_}")
    lgat = getattr(B, f"lgat{p_}")
    pA, pB = psum["pA"], psum["pB"]

    P.op("t", lambda e: e.matmul(
        pB[:, 384:385], B.gate16[0:1, cs], B.ones16[0:1, 0:1],
        start=True, stop=True),
        reads=[B.gate16, B.ones16], writes=[pB], after=deps)
    P.op("v", lambda e: e.tensor_copy(B.gcol[:, jc], pB[:, 384:385]),
         reads=[pB], writes=[B.gcol])
    P.op("t", lambda e: e.matmul(
        pB[:, 256:384], B.ones16[0:1, :], B.gate16[0:1, cs],
        start=True, stop=True),
        reads=[B.ones16, B.gate16], writes=[pB], after=deps)
    P.op("v", lambda e: e.tensor_copy(w1[:], pB[:, 256:384]),
         reads=[pB], writes=[w1])

    def one_state(st):
        la_buf = B.laf if st == "f" else B.las
        K_H = K_HORNER_F if st == "f" else K_HORNER_S
        kt = getattr(B, f"kt_{st}")
        qth = getattr(B, f"qth_{st}")
        qtp = getattr(B, f"qtp_{st}")
        ct2 = getattr(B, f"ct2_{st}")
        tt = getattr(B, f"tt_{st}")
        aqc = getattr(B, f"aqc_{st}")
        tat = getattr(B, f"tat_{st}")
        aqp = getattr(B, f"aqp_{st}")
        dend = getattr(B, f"dend_{st}")
        eh = getattr(B, f"eh_{st}")
        lam = getattr(B, f"lam_{st}")
        t1 = getattr(B, f"t1_{st}")
        t2 = getattr(B, f"t2_{st}")
        hcol = getattr(B, f"hcol_{st}")
        ct16 = getattr(B, f"ct16_{p_}{st}")
        lraw = getattr(B, f"lraw{p_}{st}")
        zh = [getattr(B, f"zh0_{p_}{st}"), getattr(B, f"zh1_{p_}{st}")]

        P.op("v", lambda e, la=la_buf: e.tensor_tensor_scan(
            lam[:], la[:, cs], B.zeros[:], 0.0, op0=ALU.add, op1=ALU.add),
            reads=[la_buf, B.zeros], writes=[lam], after=deps)
        P.op("s", lambda e, d=dend: e.activation(
            d[:, jc], lam[:, 127:128], AF.Exp), reads=[lam], writes=[dend])
        P.op("s", lambda e, h=eh: e.activation(
            h[:, jc], lam[:, 127:128], AF.Exp, scale=f32(0.5)),
            reads=[lam], writes=[eh])
        P.op("s", lambda e: e.mul(hcol[:, 0:1], lam[:, 127:128], f32(-0.5)),
             reads=[lam], writes=[hcol])
        P.op("s", lambda e: e.mul(hcol[:, 1:2], lam[:, 127:128], f32(0.5)),
             reads=[lam], writes=[hcol])
        P.op("v", lambda e, la=la_buf: e.tensor_tensor(
            t1[:], lam[:], la[:, cs], op=ALU.subtract),
            reads=[lam, la_buf], writes=[t1])
        P.op("s", lambda e: e.activation(
            t1[:], t1[:], AF.Exp, bias=hcol[:, 0:1], scale=f32(1.0)),
            reads=[t1, hcol], writes=[t1])
        P.op("v", lambda e, o=kt: e.tensor_tensor(
            o[:, cs], t1[:], B.khat[:, cs], op=ALU.mult),
            reads=[t1, B.khat], writes=[kt], after=deps)
        P.op("v", lambda e, o=qth: e.tensor_tensor(
            o[:, cs], t1[:], B.qhat[:, cs], op=ALU.mult),
            reads=[t1, B.qhat], writes=[qth], after=deps)
        P.op("s", lambda e, la=la_buf: e.activation(
            t2[:], la[:, cs], AF.Exp), reads=[la_buf], writes=[t2],
            after=deps)
        P.op("v", lambda e: e.tensor_tensor(
            t1[:], t1[:], t2[:], op=ALU.mult), reads=[t1, t2], writes=[t1])
        P.op("v", lambda e, o=qtp: e.tensor_tensor(
            o[:, cs], t1[:], B.qhat[:, cs], op=ALU.mult),
            reads=[t1, B.qhat], writes=[qtp], after=deps)
        P.op("v", lambda e: e.tensor_scalar(
            t2[:], t2[:], -1.0, -1.0,
            op0=ALU.mult, op1=ALU.subtract), reads=[t2], writes=[t2])
        P.op("s", lambda e: e.activation(
            t1[:], lam[:], AF.Exp, bias=hcol[:, 1:2], scale=f32(-1.0)),
            reads=[lam, hcol], writes=[t1])
        P.op("v", lambda e: e.tensor_tensor(
            t1[:], t1[:], t2[:], op=ALU.mult), reads=[t1, t2], writes=[t1])
        P.op("v", lambda e: e.tensor_tensor(
            t1[:], t1[:], B.khat[:, cs], op=ALU.mult),
            reads=[t1, B.khat], writes=[t1], after=deps)
        P.op("v", lambda e: e.tensor_copy(ct16[:], t1[:]),
             reads=[t1], writes=[ct16])
        P.op("v", lambda e, h=eh: e.tensor_single_scalar(
            t1[:], t1[:], h[:, jc], op=ALU.mult), reads=[t1, eh], writes=[t1])
        P.op("t", lambda e: e.transpose(pB[:, 0:128], t1[:], B.eye[:]),
             reads=[t1, B.eye], writes=[pB])
        P.op("v", lambda e, o=ct2: e.tensor_copy(o[:, cs], pB[:, 0:128]),
             reads=[pB], writes=[ct2])
        P.op("t", lambda e, a=kt: e.matmul(
            pA[:, 0:128], a[:, cs], ct16[:], start=True, stop=True),
            reads=[kt, ct16], writes=[pA])
        P.op("v", lambda e: e.select(
            lraw[:], B.masklo8[:], pA[:, 0:128], B.zeros[:], add_drain=True),
            reads=[B.masklo8, pA, B.zeros], writes=[lraw])
        if st == "s":
            P.op("v", lambda e: e.tensor_tensor(
                lgat[:], lraw[:], w1[:], op=ALU.mult),
                reads=[lraw, w1], writes=[lgat])
            lmat = lgat
        else:
            lmat = lraw
        P.op("t", lambda e, a=qth: e.matmul(
            pA[:, 128:256], ct16[:], a[:, cs], start=True, stop=True),
            reads=[ct16, qth], writes=[pA])
        P.op("v", lambda e, o=aqc: e.select(
            o[:, cs], B.maskup8[:], pA[:, 128:256], B.zeros[:],
            add_drain=True),
            reads=[B.maskup8, pA, B.zeros], writes=[aqc])
        P.op("t", lambda e, a=qtp: e.matmul(
            pA[:, 256:384], ct16[:], a[:, cs], start=True, stop=True),
            reads=[ct16, qtp], writes=[pA])
        P.op("v", lambda e, o=aqp: e.select(
            o[:, cs], B.maskup08[:], pA[:, 256:384], B.zeros[:],
            add_drain=True),
            reads=[B.maskup08, pA, B.zeros], writes=[aqp])
        if st == "s":
            P.op("v", lambda e, o=aqc: e.tensor_single_scalar(
                o[:, cs], o[:, cs], B.gcol[:, jc], op=ALU.mult),
                reads=[aqc, B.gcol], writes=[aqc])
            P.op("v", lambda e, o=aqp: e.tensor_single_scalar(
                o[:, cs], o[:, cs], B.gcol[:, jc], op=ALU.mult),
                reads=[aqp, B.gcol], writes=[aqp])
        preg = slice(128 * (1 + p_), 128 * (2 + p_))
        P.op("v", lambda e, lm=lmat: e.tensor_tensor(
            zh[0][:], B.eye[:], lm[:], op=ALU.subtract),
            reads=[B.eye, lmat], writes=[zh[0]])
        cur, nxt = 0, 1
        for _ in range(K_H - 1):
            P.op("t", lambda e, lm=lmat, c=cur: e.matmul(
                pB[:, preg], lm[:], zh[c][:], start=True, stop=True),
                reads=[lmat, zh[cur]], writes=[pB])
            P.op("v", lambda e, n=nxt: e.tensor_tensor(
                zh[n][:], B.eye[:], pB[:, preg], op=ALU.subtract),
                reads=[B.eye, pB], writes=[zh[nxt]])
            cur, nxt = nxt, cur
        P.op("v", lambda e, o=tt, c=cur: e.tensor_copy(o[:, cs], zh[c][:]),
             reads=[zh[cur]], writes=[tt])
        P.op("t", lambda e, o=tt: e.matmul(
            pB[:, preg], lraw[:], o[:, cs], start=True, stop=True),
            reads=[lraw, tt], writes=[pB])
        P.op("v", lambda e, o=tat: e.tensor_copy(o[:, cs], pB[:, preg]),
             reads=[pB], writes=[tat])

    one_state("f")
    one_state("s")


def emit_relay_lite(P, B, psum, j, state_deps=()):
    cs = slice(j * 128, (j + 1) * 128)
    jc = slice(j, j + 1)
    cur, nxt = j % 2, 1 - j % 2
    g0f_c, g0f_n = B.g0f[cur], B.g0f[nxt]
    g0s_c, g0s_n = B.g0s[cur], B.g0s[nxt]
    pR, pG, pO = psum["pR"], psum["pG"], psum["pO"]
    sd = list(state_deps)

    P.op("v", lambda e: e.tensor_single_scalar(
        B.g0h16f[:], g0f_c[:], B.eh_f[:, jc], op=ALU.mult),
        reads=[g0f_c, B.eh_f], writes=[B.g0h16f], after=sd)
    P.op("v", lambda e: e.tensor_single_scalar(
        B.g0h16s[:], g0s_c[:], B.eh_s[:, jc], op=ALU.mult),
        reads=[g0s_c, B.eh_s], writes=[B.g0h16s], after=sd)
    P.op("t", lambda e: e.matmul(
        pR[:, 0:128], B.kt_f[:, cs], B.g0h16f[:], start=True, stop=True),
        reads=[B.kt_f, B.g0h16f], writes=[pR])
    P.op("t", lambda e: e.matmul(
        pR[:, 128:256], B.kt_s[:, cs], B.g0h16s[:], start=True, stop=True),
        reads=[B.kt_s, B.g0h16s], writes=[pR])
    P.op("v", lambda e: e.tensor_tensor(
        B.bv16f[:], B.vc_tm[:, cs], pR[:, 0:128], op=ALU.subtract),
        reads=[B.vc_tm, pR], writes=[B.bv16f])
    P.op("v", lambda e: e.tensor_tensor(
        B.bv16s[:], B.vc_tm[:, cs], pR[:, 128:256], op=ALU.subtract),
        reads=[B.vc_tm, pR], writes=[B.bv16s])
    P.op("t", lambda e: e.matmul(
        pG[:, 0:128], B.tt_f[:, cs], B.bv16f[:], start=True, stop=True),
        reads=[B.tt_f, B.bv16f], writes=[pG])
    P.op("t", lambda e: e.matmul(
        pG[:, 128:256], B.tt_s[:, cs], B.bv16s[:], start=True, stop=True),
        reads=[B.tt_s, B.bv16s], writes=[pG])
    P.op("v", lambda e: e.tensor_copy(B.m0w[:], pG[:, 0:256]),
         reads=[pG], writes=[B.m0w])
    P.op("t", lambda e: e.matmul(
        pG[:, 256:384], B.ct2_f[:, cs], B.m0w[:, 0:128],
        start=True, stop=True),
        reads=[B.ct2_f, B.m0w], writes=[pG])
    P.op("v", lambda e: e.scalar_tensor_tensor(
        g0f_n[:], g0f_c[:], B.dend_f[:, jc], pG[:, 256:384],
        op0=ALU.mult, op1=ALU.add),
        reads=[g0f_c, B.dend_f, pG], writes=[g0f_n])
    P.op("t", lambda e: e.matmul(
        pG[:, 384:512], B.ct2_s[:, cs], B.m0w[:, 128:256],
        start=True, stop=True),
        reads=[B.ct2_s, B.m0w], writes=[pG])
    P.op("v", lambda e: e.scalar_tensor_tensor(
        g0s_n[:], g0s_c[:], B.dend_s[:, jc], pG[:, 384:512],
        op0=ALU.mult, op1=ALU.add),
        reads=[g0s_c, B.dend_s, pG], writes=[g0s_n])

    P.op("t", lambda e: e.matmul(
        pR[:, 256:384], B.qth_f[:, cs], B.g0h16f[:], start=True, stop=False),
        reads=[B.qth_f, B.g0h16f], writes=[pR])
    P.op("t", lambda e: e.matmul(
        pR[:, 256:384], B.qth_s[:, cs], B.g0h16s[:], start=False, stop=False),
        reads=[B.qth_s, B.g0h16s], writes=[pR])
    P.op("t", lambda e: e.matmul(
        pR[:, 256:384], B.aqc_f[:, cs], B.m0w[:, 0:128],
        start=False, stop=False),
        reads=[B.aqc_f, B.m0w], writes=[pR])
    P.op("t", lambda e: e.matmul(
        pR[:, 256:384], B.aqc_s[:, cs], B.m0w[:, 128:256],
        start=False, stop=True),
        reads=[B.aqc_s, B.m0w], writes=[pR])
    P.op("v", lambda e: e.scalar_tensor_tensor(
        B.dbuf[:], pR[:, 256:384], -0.5, B.vc_tm[:, cs],
        op0=ALU.mult, op1=ALU.add), reads=[pR, B.vc_tm], writes=[B.dbuf])
    P.op("s", lambda e: e.activation(
        B.sqs[:], B.dbuf[:], AF.Square, accum_out=B.scol[:, 0:1]),
        reads=[B.dbuf], writes=[B.sqs, B.scol])
    P.op("s", lambda e: e.activation(
        B.scol[:, 1:2], B.scol[:, 0:1], AF.Exp,
        scale=f32(-1.0 / (1.0 + 1e-6))), reads=[B.scol], writes=[B.scol])
    P.op("v", lambda e: e.tensor_single_scalar(
        B.scol[:, 1:2], B.scol[:, 1:2], 1.0, op=ALU.add),
        reads=[B.scol], writes=[B.scol])
    P.op("v", lambda e: e.reciprocal(B.scol8[:, jc], B.scol[:, 1:2]),
        reads=[B.scol], writes=[B.scol8])
    P.op("v", lambda e: e.tensor_single_scalar(
        B.ucol8[:, jc], B.scol8[:, jc], 0.7, op=ALU.is_gt),
        reads=[B.scol8], writes=[B.ucol8])
    P.op("v", lambda e: e.tensor_single_scalar(
        B.scol[:, 6:7], B.ucol8[:, jc], float(np.log(0.9)), op=ALU.mult),
        reads=[B.ucol8], writes=[B.scol])
    P.op("v", lambda e: e.tensor_copy(B.pcol16[:, 0:1], B.scol[:, 6:7]),
         reads=[B.scol], writes=[B.pcol16])
    P.op("t", lambda e: e.matmul(
        pO[:, 384:385], B.up016[:], B.pcol16[:, 0:1], start=True, stop=True),
        reads=[B.up016, B.pcol16], writes=[pO])
    P.op("s", lambda e: e.activation(
        B.pcol8[:, jc], pO[:, 384:385], AF.Exp), reads=[pO],
        writes=[B.pcol8])
    P.op("s", lambda e: e.activation(
        B.ipcol8[:, jc], pO[:, 384:385], AF.Exp, scale=f32(-1.0)),
        reads=[pO], writes=[B.ipcol8])
    P.op("t", lambda e: e.matmul(
        pO[0:1, 255:256], B.pcol16[:, 0:1], B.ones16[:, 0:1],
        start=True, stop=True),
        reads=[B.pcol16, B.ones16], writes=[pO])
    P.op("s", lambda e: e.activation(
        B.nsab[0:1, 1:2], pO[0:1, 255:256], AF.Exp),
        reads=[pO], writes=[B.nsab])
    P.op("v", lambda e: e.tensor_tensor(
        B.scol[:, 7:8], B.ucol8[:, jc], B.ipcol8[:, jc], op=ALU.mult),
        reads=[B.ucol8, B.ipcol8], writes=[B.scol])
    P.op("v", lambda e: e.tensor_single_scalar(
        B.zw[:, cs], B.zw[:, cs], B.scol[:, 7:8], op=ALU.mult),
        reads=[B.zw, B.scol], writes=[B.zw])
    P.op("t", lambda e: e.matmul(
        pO[0:1, 256:384], B.ones16[:, 0:1], B.zw[:, cs],
        start=True, stop=True),
        reads=[B.ones16, B.zw], writes=[pO])
    P.op("v", lambda e: e.tensor_single_scalar(
        B.nsab[0:1, 0:1], B.nsab[0:1, 0:1], B.nsab[0:1, 1:2], op=ALU.mult),
        reads=[B.nsab], writes=[B.nsab])
    P.op("v", lambda e: e.tensor_tensor(
        B.nsab[0:1, 4:132], B.nsab[0:1, 4:132], pO[0:1, 256:384],
        op=ALU.add), reads=[B.nsab, pO], writes=[B.nsab])
    P.op("v", lambda e: e.tensor_single_scalar(
        B.nsab[0:1, 4:132], B.nsab[0:1, 4:132], B.nsab[0:1, 1:2],
        op=ALU.mult), reads=[B.nsab], writes=[B.nsab])


def emit_tchain(P, B, pb, j):
    cs = slice(j * 128, (j + 1) * 128)
    jc = slice(j, j + 1)
    for si, st in enumerate(("f", "s")):
        kt = getattr(B, "kt_" + st)
        tt = getattr(B, "tt_" + st)
        ct2 = getattr(B, "ct2_" + st)
        eh = getattr(B, "eh_" + st)
        dend = getattr(B, "dend_" + st)
        ta = slice(128 * (si + 1), 128 * (si + 2))
        t16 = slice(128 * si, 128 * (si + 1))
        P.op("v", lambda e, eh=eh: e.tensor_single_scalar(
            B.deh16[:], B.eye[:], eh[:, jc], op=ALU.mult),
            reads=[B.eye, eh], writes=[B.deh16])
        P.op("t", lambda e, kt=kt: e.matmul(
            pb[0][:, 0:128], kt[:, cs], B.deh16[:], start=True, stop=True),
            reads=[kt, B.deh16], writes=[pb[0]])
        P.op("v", lambda e: e.tensor_copy(B.tch16[:], pb[0][:, 0:128]),
             reads=[pb[0]], writes=[B.tch16])
        P.op("t", lambda e, tt=tt: e.matmul(
            pb[0][:, 128:256], tt[:, cs], B.tch16[:], start=True, stop=True),
            reads=[tt, B.tch16], writes=[pb[0]])
        P.op("v", lambda e: e.tensor_copy(B.tch16[:], pb[0][:, 128:256]),
             reads=[pb[0]], writes=[B.tch16])
        P.op("t", lambda e, ct2=ct2: e.matmul(
            pb[0][:, 256:384], ct2[:, cs], B.tch16[:], start=True, stop=True),
            reads=[ct2, B.tch16], writes=[pb[0]])
        if j == 0:
            P.op("v", lambda e, dend=dend, ta=ta: e.scalar_tensor_tensor(
                B.prj_sq[:, ta], B.eye[:], dend[:, jc], pb[0][:, 256:384],
                op0=ALU.mult, op1=ALU.subtract),
                reads=[B.eye, dend, pb[0]], writes=[B.prj_sq])
        else:
            P.op("v", lambda e: e.tensor_copy(
                B.prj_sq[:, 0:128], pb[0][:, 256:384]),
                reads=[pb[0]], writes=[B.prj_sq])
            P.op("t", lambda e: e.transpose(
                pb[0][:, 384:512], B.prj_sq[:, 0:128], B.eye[:]),
                reads=[B.prj_sq, B.eye], writes=[pb[0]])
            P.op("v", lambda e: e.tensor_copy(B.pjt16[:], pb[0][:, 384:512]),
                 reads=[pb[0]], writes=[B.pjt16])
            P.op("t", lambda e, t16=t16: e.matmul(
                pb[1][:, 0:128], B.pjt16[:], B.girs16[:, t16],
                start=True, stop=True),
                reads=[B.pjt16, B.girs16], writes=[pb[1]])
            P.op("v", lambda e, dend=dend, ta=ta: e.scalar_tensor_tensor(
                B.prj_sq[:, ta], B.prj_sq[:, ta], dend[:, jc],
                pb[1][:, 0:128], op0=ALU.mult, op1=ALU.subtract),
                reads=[B.prj_sq, dend, pb[1]], writes=[B.prj_sq])
        P.op("v", lambda e, ta=ta, t16=t16: e.tensor_copy(
            B.girs16[:, t16], B.prj_sq[:, ta]),
            reads=[B.prj_sq], writes=[B.girs16])


def emit_replay_chunk(P, B, psum, j, state_deps=()):
    cs = slice(j * 128, (j + 1) * 128)
    jc = slice(j, j + 1)
    cur, nxt = j % 2, 1 - j % 2
    g0f_c, g0f_n = B.g0f[cur], B.g0f[nxt]
    g0s_c, g0s_n = B.g0s[cur], B.g0s[nxt]
    ns_c, ns_n = B.ns0r[cur], B.ns0r[nxt]
    pR, pG, pO = psum["pR"], psum["pG"], psum["pO"]
    sd = list(state_deps)

    P.op("v", lambda e: e.tensor_single_scalar(
        B.g0h16f[:], g0f_c[:], B.eh_f[:, jc], op=ALU.mult),
        reads=[g0f_c, B.eh_f], writes=[B.g0h16f], after=sd)
    P.op("v", lambda e: e.tensor_single_scalar(
        B.g0h16s[:], g0s_c[:], B.eh_s[:, jc], op=ALU.mult),
        reads=[g0s_c, B.eh_s], writes=[B.g0h16s], after=sd)
    P.op("t", lambda e: e.matmul(
        pR[:, 0:128], B.kt_f[:, cs], B.g0h16f[:], start=True, stop=True),
        reads=[B.kt_f, B.g0h16f], writes=[pR])
    P.op("t", lambda e: e.matmul(
        pR[:, 128:256], B.kt_s[:, cs], B.g0h16s[:], start=True, stop=True),
        reads=[B.kt_s, B.g0h16s], writes=[pR])
    P.op("v", lambda e: e.tensor_tensor(
        B.bv16f[:], B.vc_tm[:, cs], pR[:, 0:128], op=ALU.subtract),
        reads=[B.vc_tm, pR], writes=[B.bv16f])
    P.op("v", lambda e: e.tensor_tensor(
        B.bv16s[:], B.vc_tm[:, cs], pR[:, 128:256], op=ALU.subtract),
        reads=[B.vc_tm, pR], writes=[B.bv16s])
    P.op("t", lambda e: e.matmul(
        pG[:, 0:128], B.tt_f[:, cs], B.bv16f[:], start=True, stop=True),
        reads=[B.tt_f, B.bv16f], writes=[pG])
    P.op("v", lambda e: e.tensor_copy(B.m0w[:, 0:128], pG[:, 0:128]),
         reads=[pG], writes=[B.m0w])
    P.op("t", lambda e: e.matmul(
        pG[:, 128:256], B.tt_s[:, cs], B.bv16s[:], start=True, stop=True),
        reads=[B.tt_s, B.bv16s], writes=[pG])
    P.op("v", lambda e: e.tensor_copy(B.m0w[:, 128:256], pG[:, 128:256]),
         reads=[pG], writes=[B.m0w])
    P.op("t", lambda e: e.matmul(
        pR[:, 256:384], B.qth_f[:, cs], B.g0h16f[:], start=True, stop=False),
        reads=[B.qth_f, B.g0h16f], writes=[pR])
    P.op("t", lambda e: e.matmul(
        pR[:, 256:384], B.qth_s[:, cs], B.g0h16s[:], start=False, stop=False),
        reads=[B.qth_s, B.g0h16s], writes=[pR])
    P.op("t", lambda e: e.matmul(
        pR[:, 256:384], B.aqc_f[:, cs], B.m0w[:, 0:128], start=False, stop=False),
        reads=[B.aqc_f, B.m0w], writes=[pR])
    P.op("t", lambda e: e.matmul(
        pR[:, 256:384], B.aqc_s[:, cs], B.m0w[:, 128:256], start=False, stop=True),
        reads=[B.aqc_s, B.m0w], writes=[pR])
    P.op("v", lambda e: e.tensor_copy(B.scol[:, 1:2], B.scol8[:, jc]),
         reads=[B.scol8], writes=[B.scol])
    P.op("v", lambda e: e.tensor_single_scalar(
        B.scol[:, 2:3], B.scol[:, 1:2], 1.0, op=ALU.subtract),
        reads=[B.scol], writes=[B.scol])
    P.op("v", lambda e: e.tensor_tensor(
        B.scol[:, 3:4], B.scol[:, 2:3], B.gcol[:, jc], op=ALU.mult),
        reads=[B.scol, B.gcol], writes=[B.scol])
    P.op("v", lambda e: e.tensor_tensor(
        B.scol[:, 4:5], B.scol[:, 1:2], B.gcol[:, jc], op=ALU.mult),
        reads=[B.scol, B.gcol], writes=[B.scol])
    P.op("v", lambda e: e.tensor_single_scalar(
        B.dm16[:], B.m0w[:, 0:128], B.scol[:, 2:3], op=ALU.mult),
        reads=[B.m0w, B.scol], writes=[B.dm16])
    P.op("t", lambda e: e.matmul(
        pG[:, 256:384], B.tat_f[:, cs], B.dm16[:], start=True, stop=True),
        reads=[B.tat_f, B.dm16], writes=[pG])
    P.op("v", lambda e: e.tensor_tensor(
        B.m1f[:], B.m0w[:, 0:128], pG[:, 256:384], op=ALU.subtract),
        reads=[B.m0w, pG], writes=[B.m1f])
    P.op("v", lambda e: e.tensor_single_scalar(
        B.yf[:], B.m1f[:], B.scol[:, 1:2], op=ALU.mult),
        reads=[B.m1f, B.scol], writes=[B.yf])
    P.op("v", lambda e: e.tensor_single_scalar(
        B.dm16[:], B.m0w[:, 128:256], B.scol[:, 3:4], op=ALU.mult),
        reads=[B.m0w, B.scol], writes=[B.dm16])
    P.op("t", lambda e: e.matmul(
        pG[:, 256:384], B.tat_s[:, cs], B.dm16[:], start=True, stop=True),
        reads=[B.tat_s, B.dm16], writes=[pG])
    P.op("v", lambda e: e.tensor_tensor(
        B.m1s[:], B.m0w[:, 128:256], pG[:, 256:384], op=ALU.subtract),
        reads=[B.m0w, pG], writes=[B.m1s])
    P.op("v", lambda e: e.tensor_single_scalar(
        B.ys[:], B.m1s[:], B.scol[:, 4:5], op=ALU.mult),
        reads=[B.m1s, B.scol], writes=[B.ys])
    P.op("t", lambda e: e.matmul(
        pG[:, 0:128], B.ct2_f[:, cs], B.yf[:], start=True, stop=True),
        reads=[B.ct2_f, B.yf], writes=[pG])
    P.op("v", lambda e: e.scalar_tensor_tensor(
        g0f_n[:], g0f_c[:], B.dend_f[:, jc], pG[:, 0:128],
        op0=ALU.mult, op1=ALU.add),
        reads=[g0f_c, B.dend_f, pG], writes=[g0f_n])
    P.op("t", lambda e: e.matmul(
        pG[:, 128:256], B.ct2_s[:, cs], B.ys[:], start=True, stop=True),
        reads=[B.ct2_s, B.ys], writes=[pG])
    P.op("v", lambda e: e.scalar_tensor_tensor(
        g0s_n[:], g0s_c[:], B.dend_s[:, jc], pG[:, 128:256],
        op0=ALU.mult, op1=ALU.add),
        reads=[g0s_c, B.dend_s, pG], writes=[g0s_n])

    P.op("v", lambda e, r=ns_c: e.tensor_copy(r[0:1, :], B.nsab[0:1, 4:132]),
         reads=[B.nsab], writes=[ns_c])
    P.op("t", lambda e: e.matmul(
        pO[:, 256:384], B.up016[:], B.zw[:, cs], start=True, stop=False),
        reads=[B.up016, B.zw], writes=[pO])
    P.op("t", lambda e, r=ns_c: e.matmul(
        pO[:, 256:384], B.ones16[0:1, :], r[0:1, :], start=False, stop=True),
        reads=[B.ones16, ns_c], writes=[pO], after=sd)
    tnsc = P.op("v", lambda e: e.tensor_single_scalar(
        B.nsc[:, cs], pO[:, 256:384], B.pcol8[:, jc], op=ALU.mult),
        reads=[pO, B.pcol8], writes=[B.nsc])
    P.op("v", lambda e: e.tensor_copy(B.pcol16[:, 0:1], B.ucol8[:, jc]),
         reads=[B.ucol8], writes=[B.pcol16])
    P.op("t", lambda e: e.matmul(
        pO[0:1, 255:256], B.pcol16[:, 0:1], B.ones16[:, 0:1],
        start=True, stop=True),
        reads=[B.pcol16, B.ones16], writes=[pO])
    P.op("s", lambda e: e.activation(
        B.nsab[0:1, 1:2], pO[0:1, 255:256], AF.Exp,
        scale=f32(np.log(0.9))), reads=[pO], writes=[B.nsab])
    P.op("t", lambda e: e.matmul(
        pO[0:1, 112:240], B.ones16[:, 0:1], B.zw[:, cs], start=True,
        stop=True),
        reads=[B.ones16, B.zw], writes=[pO])
    P.op("v", lambda e: e.tensor_tensor(
        B.nsab[0:1, 4:132], B.nsab[0:1, 4:132], pO[0:1, 112:240],
        op=ALU.add), reads=[B.nsab, pO], writes=[B.nsab])
    P.op("v", lambda e: e.tensor_single_scalar(
        B.nsab[0:1, 4:132], B.nsab[0:1, 4:132], B.nsab[0:1, 1:2],
        op=ALU.mult), reads=[B.nsab], writes=[B.nsab])

    P.op("t", lambda e: e.matmul(
        pO[:, 0:128], B.qtp_f[:, cs], B.g0h16f[:], start=True, stop=False),
        reads=[B.qtp_f, B.g0h16f], writes=[pO])
    P.op("t", lambda e: e.matmul(
        pO[:, 0:128], B.aqp_f[:, cs], B.yf[:], start=False, stop=True),
        reads=[B.aqp_f, B.yf], writes=[pO])
    P.op("t", lambda e: e.matmul(
        pO[:, 128:256], B.qtp_s[:, cs], B.g0h16s[:], start=True, stop=False),
        reads=[B.qtp_s, B.g0h16s], writes=[pO])
    P.op("t", lambda e: e.matmul(
        pO[:, 128:256], B.aqp_s[:, cs], B.ys[:], start=False, stop=True),
        reads=[B.aqp_s, B.ys], writes=[pO])
    P.op("v", lambda e: e.tensor_single_scalar(
        B.out_tm[:, cs], pO[:, 0:128], B.pw[:, 3 * j:3 * j + 1],
        op=ALU.mult), reads=[pO, B.pw], writes=[B.out_tm])
    P.op("v", lambda e: e.scalar_tensor_tensor(
        B.out_tm[:, cs], pO[:, 128:256], B.pw[:, 3 * j + 1:3 * j + 2],
        B.out_tm[:, cs], op0=ALU.mult, op1=ALU.add),
        reads=[pO, B.pw, B.out_tm], writes=[B.out_tm])
    P.op("v", lambda e: e.scalar_tensor_tensor(
        B.out_tm[:, cs], B.nsc[:, cs], B.pw[:, 3 * j + 2:3 * j + 3],
        B.out_tm[:, cs], op0=ALU.mult, op1=ALU.add),
        reads=[B.nsc, B.pw, B.out_tm], writes=[B.out_tm])
    P.op("s", lambda e: e.activation(
        B.sqs[:], B.out_tm[:, cs], AF.Square, accum_out=B.rmsc[:, 0:1]),
        reads=[B.out_tm], writes=[B.sqs, B.rmsc])
    P.op("s", lambda e: e.activation(
        B.rmsc[:, 1:2], B.rmsc[:, 0:1], AF.Sqrt,
        scale=f32(1.0 / 128.0), bias=f32(1e-6)),
        reads=[B.rmsc], writes=[B.rmsc])
    P.op("v", lambda e: e.reciprocal(B.rmsc[:, 1:2], B.rmsc[:, 1:2]),
         reads=[B.rmsc], writes=[B.rmsc])
    P.op("v", lambda e: e.tensor_single_scalar(
        B.sqs[:], B.out_tm[:, cs], B.rmsc[:, 1:2], op=ALU.mult),
        reads=[B.out_tm, B.rmsc], writes=[B.sqs])
    P.op("v", lambda e: e.tensor_tensor(
        B.dbuf[:], B.sqs[:], B.nrmw[:], op=ALU.mult),
        reads=[B.sqs, B.nrmw], writes=[B.dbuf])
    P.op("t", lambda e: e.transpose(pO[:, 256:384], B.dbuf[:], B.eye[:]),
         reads=[B.dbuf, B.eye], writes=[pO])
    P.op("v", lambda e: e.tensor_copy(B.o_vm[:, cs], pO[:, 256:384]),
         reads=[pO], writes=[B.o_vm])


_CACHE = {}


def build():
    nc = bass.Bass("TRN2", target_bir_lowering=False)
    consts = np_consts()

    def din(name, shape, dt=F32):
        return nc.dram_tensor(name, shape, dt, kind="ExternalInput").ap()

    x = din("x", [T, D])
    wk = din("wk", [D, 128])
    wv = din("wv", [D, 128])
    wq = din("wq", [D, 128])
    waf = din("waf", [D, 128])
    was = din("was", [D, 128])
    baf = din("baf", [128])
    bas = din("bas", [128])
    flam = din("flam", [128])
    slam = din("slam", [128])
    nmw1 = din("nmw1", [128, 256])
    nmb1 = din("nmb1", [256])
    nmw2 = din("nmw2", [256, 128])
    nmb2 = din("nmb2", [128])
    irw1 = din("irw1", [D, 128])
    irb1 = din("irb1", [128])
    irw2 = din("irw2", [128, 1])
    irb2 = din("irb2", [1])
    pww = din("pww", [D, 3])
    pwb = din("pwb", [3])
    wo = din("wo", [128, D])
    nrm = din("nrm", [128])
    w1s = din("w1s", [D // 8, H])
    w2s = din("w2s", [D // 8, H])
    w3s = din("w3s", [H // 8, D])
    rsel = din("rsel", [128, 8])
    rsel2 = din("rsel2", [1, 8])
    cins = {n: din(n, list(consts[n].shape)) for n in CONST_NAMES}
    for n in CONST_NAMES_U8:
        cins[n] = din(n, [128, 128], U8)
    for n in CONST_NAMES_16:
        cins[n] = din(n, list(consts[n].shape), BF16)

    y_out = nc.dram_tensor("y", [T, D], F32, kind="ExternalOutput").ap()

    xnsc = nc.dram_tensor("xnsc", [T, D], BF16, kind="Internal").ap()
    cc_gin = nc.dram_tensor("cc_gin", [1, 4], F32, kind="Internal").ap()
    cc_gout = nc.dram_tensor("cc_gout", [1, 4], F32, kind="Internal").ap()
    cc_w1in = nc.dram_tensor("cc_w1in", [D // 8, H], BF16,
                             kind="Internal").ap()
    cc_w1out = nc.dram_tensor("cc_w1out", [D, H], BF16, kind="Internal").ap()
    cc_w2in = nc.dram_tensor("cc_w2in", [D // 8, H], BF16,
                             kind="Internal").ap()
    cc_w2out = nc.dram_tensor("cc_w2out", [D, H], BF16, kind="Internal").ap()
    cc_w3in = nc.dram_tensor("cc_w3in", [H // 8, D], BF16,
                             kind="Internal").ap()
    cc_w3out = nc.dram_tensor("cc_w3out", [H, D], BF16, kind="Internal").ap()
    cc_rin = nc.dram_tensor("cc_rin", [128, 512], F32, kind="Internal").ap()
    cc_rout = nc.dram_tensor("cc_rout", [1024, 512], F32,
                             kind="Internal").ap()
    cc_nsin = nc.dram_tensor("cc_nsin", [1, 132], F32, kind="Internal").ap()
    cc_nsout = nc.dram_tensor("cc_nsout", [8, 132], F32,
                              kind="Internal").ap()

    with ExitStack() as stack:
        B = Bufs(nc, stack)
        P = Prog()
        pb = []
        for i in range(7):
            h = stack.enter_context(nc.psum_tensor(f"pp{i}", [128, 512], F32))
            P.mark_excl(h)
            pb.append(h)
        h = stack.enter_context(nc.psum_tensor("pp7", [128, 512], F32))
        P.mark_excl(h)
        pb.append(h)
        psum = {"pA": pb[4], "pB": pb[5], "pR": pb[6], "pG": pb[7],
                "pO": pb[4]}
        sems = {k: stack.enter_context(nc.semaphore("sem_" + k))
                for k in Prog.ENGINES}
        sems["ya"] = stack.enter_context(nc.semaphore("sem_ya"))
        sems["yb"] = stack.enter_context(nc.semaphore("sem_yb"))
        sems["za"] = stack.enter_context(nc.semaphore("sem_za"))
        sems["zb"] = stack.enter_context(nc.semaphore("sem_zb"))
        sems["w"] = stack.enter_context(nc.semaphore("sem_w"))
        cc_sem = stack.enter_context(nc.semaphore("sem_c"))
        sems["c"] = cc_sem
        cc2_sem = stack.enter_context(nc.semaphore("sem_c2"))
        sems["c2"] = cc2_sem
        sems["z"] = stack.enter_context(nc.semaphore("sem_z"))
        block = stack.enter_context(nc.Block())
        regs = {}

        def getreg(e, name):
            if name not in regs:
                regs[name] = e.alloc_register(name)
            return regs[name]

        for n in CONST_NAMES + CONST_NAMES_U8 + CONST_NAMES_16:
            tgt = {"c_eye": B.eye, "c_ones": B.ones,
                   "c_zeros": B.zeros,
                   "c_masklo8": B.masklo8, "c_maskup8": B.maskup8,
                   "c_maskup08": B.maskup08, "c_eye16": B.eye16,
                   "c_ones16": B.ones16, "c_up016": B.up016,
                   "c_e12716": B.e12716}[n]
            P.op("y", lambda e, b=tgt, a=cins[n]: e.dma_start(
                out=b[:], in_=a[:]), writes=[tgt])
        projdefs = [(wk, 0, 0), (wv, 0, 2048), (wq, 0, 4096),
                    (waf, 0, 6144), (was, 1, 0), (irw1, 1, 2048)]
        for wsrc, bsel, col in projdefs:
            buf = B.big0a if bsel == 0 else B.big0b
            P.op("g", lambda e, s=wsrc, c=col, b=buf: e.dma_start(
                out=b.ap().rearrange(
                    "p (c d) -> p c d", d=128)[:, c // 128:c // 128 + 16, :],
                in_=s.rearrange("(c p) d -> p c d", p=128)),
                writes=[buf])
        P.op("g", lambda e: e.dma_start(
            out=B.ppw16.ap().rearrange("p (c d) -> p c d", d=3),
            in_=pww.rearrange("(c p) d -> p c d", p=128)), writes=[B.ppw16])
        P.op("g", lambda e: e.dma_start(out=B.nm1_16[:], in_=nmw1[:]),
             writes=[B.nm1_16])
        P.op("g", lambda e: e.dma_start(
            out=B.nm2_16.ap().rearrange("p (c d) -> p c d", d=128),
            in_=nmw2.rearrange("(c p) d -> p c d", p=128)), writes=[B.nm2_16])
        P.op("g", lambda e: e.dma_start(out=B.wo16[:], in_=wo[:]),
             writes=[B.wo16])
        P.op("g", lambda e: e.dma_start(out=B.irw2_16[:], in_=irw2[:]),
             writes=[B.irw2_16])
        P.op("g", lambda e: e.dma_start(out=B.nmb2r16[0:1, :],
                                        in_=nmb2.rearrange("(o p) -> o p", o=1)),
             writes=[B.nmb2r16])
        P.op("g", lambda e: e.dma_start(out=B.nw16r[0:1, :], in_=nrm.rearrange("(o p) -> o p", o=1)),
             writes=[B.nw16r])
        P.op("y", lambda e: e.dma_start(out=B.bafc[:], in_=baf.rearrange("(p o) -> p o", o=1)),
             writes=[B.bafc])
        P.op("y", lambda e: e.dma_start(out=B.basc[:], in_=bas.rearrange("(p o) -> p o", o=1)),
             writes=[B.basc])
        P.op("y", lambda e: e.dma_start(out=B.la8f[:], in_=flam.rearrange("(p o) -> p o", o=1)),
             writes=[B.la8f])
        P.op("y", lambda e: e.dma_start(out=B.la8s[:], in_=slam.rearrange("(p o) -> p o", o=1)),
             writes=[B.la8s])
        P.op("y", lambda e: e.dma_start(out=B.irb1c[:], in_=irb1.rearrange("(p o) -> p o", o=1)),
             writes=[B.irb1c])
        P.op("y", lambda e: e.dma_start(out=B.irb2c[:], in_=irb2.rearrange("(o p) -> o p", o=1)),
             writes=[B.irb2c])
        P.op("y", lambda e: e.dma_start(out=B.pwbc[0:3, :], in_=pwb.rearrange("(p o) -> p o", o=1)),
             writes=[B.pwbc])
        P.op("y", lambda e: e.dma_start(
            out=B.nmb1c[:],
            in_=nmb1.rearrange("(c p) -> p c", p=128),
            allow_slow_non_contiguous=True), writes=[B.nmb1c])
        P.op("y", lambda e: e.dma_start(out=B.rsel_sb[:], in_=rsel[:]),
             writes=[B.rsel_sb])
        P.op("y", lambda e: e.dma_start(out=B.rsel2_sb[:], in_=rsel2[:]),
             writes=[B.rsel2_sb])
        P.flush("y")
        P.flush("g")


        for lb in (B.la8f, B.la8s):
            P.op("s", lambda e, b=lb: e.activation(b[:], b[:], AF.Sigmoid),
                 reads=[lb], writes=[lb])
            P.op("s", lambda e, b=lb: e.add(b[:], b[:], f32(1e-8)),
                 reads=[lb], writes=[lb])
            P.op("s", lambda e, b=lb: e.activation(b[:], b[:], AF.Ln),
                 reads=[lb], writes=[lb])
            P.op("s", lambda e, b=lb: e.mul(b[:], b[:], f32(8.0)),
                 reads=[lb], writes=[lb])
        P.op("t", lambda e: e.matmul(
            pb[0][:, 0:128], B.ones16[0:1, :], B.nw16r[0:1, :],
            start=True, stop=True),
            reads=[B.ones16, B.nw16r], writes=[pb[0]])
        P.op("v", lambda e: e.tensor_copy(B.nrmw[:], pb[0][:, 0:128]),
             reads=[pb[0]], writes=[B.nrmw])
        for buf in (B.g0f[0], B.g0f[1], B.g0s[0], B.g0s[1], B.ns0r[0],
                    B.ns0r[1], B.spack, B.stload):
            P.op("v", lambda e, b=buf: e.memset(b[:], 0.0), writes=[buf])
        P.op("v", lambda e: e.tensor_scalar(
            B.rselq_sb[:], B.rsel_sb[:], -1.0, 1.0,
            op0=ALU.mult, op1=ALU.add),
            reads=[B.rsel_sb], writes=[B.rselq_sb])

        for i in range(NL):
            for hf in range(2):
                P.op("y", lambda e, i=i, hf=hf, b=B.xa[hf]: e.dma_start(
                    out=b[:],
                    in_=x[128 * i:128 * (i + 1), 1024 * hf:1024 * (hf + 1)]),
                    writes=[B.xa[hf]])
            P.flush("y")
            for q_ in range(4):
                xb = B.xa[q_ // 2]
                P.op("v", lambda e, b=xb, q_=q_: e.bn_stats(
                    B.lncols[:, 6 * q_:6 * q_ + 6],
                    b[:, 512 * (q_ % 2):512 * (q_ % 2 + 1)]),
                    reads=[xb], writes=[B.lncols])
            P.op("v", lambda e: e.bn_aggr(B.lncols[:, 24:26],
                                          B.lncols[:, 0:24]),
                 reads=[B.lncols], writes=[B.lncols])
            P.op("s", lambda e: e.activation(
                B.lncols[:, 26:27], B.lncols[:, 25:26], AF.Sqrt,
                bias=f32(1e-5)), reads=[B.lncols], writes=[B.lncols])
            P.op("v", lambda e: e.reciprocal(
                B.lncols[:, 27:28], B.lncols[:, 26:27]),
                reads=[B.lncols], writes=[B.lncols])
            for hf in range(2):
                P.op("v", lambda e, hf=hf, b=B.xa[hf]: e.tensor_scalar(
                    B.hid16[:, 4096 + 1024 * hf:4096 + 1024 * (hf + 1)],
                    b[:], B.lncols[:, 24:25],
                    B.lncols[:, 27:28], op0=ALU.subtract, op1=ALU.mult),
                    reads=[B.xa[hf], B.lncols], writes=[B.hid16])
            P.op("y", lambda e, i=i: e.dma_start(
                out=xnsc[128 * i:128 * (i + 1), :],
                in_=B.hid16[:, 4096:6144]),
                reads=[B.hid16], writes=[xnsc])
        P.flush("y")

        P.op("v", lambda e: e.memset(B.pabs[:], 0.0), writes=[B.pabs])
        wtiles = []
        for m, (wsrc, nr, ncol) in enumerate(
                [(w1s, D // 8, H), (w2s, D // 8, H), (w3s, H // 8, D)]):
            tl = []
            r0 = 0
            while r0 < nr:
                p = min(128, nr - r0)
                c0 = 0
                while c0 < ncol:
                    w = min(1024, ncol - c0)
                    tl.append((m, wsrc, r0, p, c0, w))
                    c0 += w
                r0 += p
            wtiles.append(tl)
        for tl in wtiles:
            for k, (m, wsrc, r0, p, c0, w) in enumerate(tl):
                xa = B.xa[k % 2]
                lq = "y" if k % 2 == 0 else "w"
                P.op(lq, lambda e, s=wsrc, r0=r0, p=p, c0=c0, w=w, b=xa:
                     e.dma_start(out=b[0:p, 0:w],
                                 in_=s[r0:r0 + p, c0:c0 + w]), writes=[xa])
                P.flush(lq)
                P.op("v", lambda e, p=p, w=w, b=xa, k=k: e.tensor_reduce(
                    B.lncols[0:p, 10 + (k % 2):11 + (k % 2)], b[0:p, 0:w],
                    axis=AX.X, op=ALU.add,
                    apply_absolute_value=True),
                    reads=[xa], writes=[B.lncols])
                P.op("v", lambda e, m=m, p=p, k=k: e.tensor_tensor(
                    B.pabs[0:p, m:m + 1], B.pabs[0:p, m:m + 1],
                    B.lncols[0:p, 10 + (k % 2):11 + (k % 2)], op=ALU.add),
                    reads=[B.pabs, B.lncols], writes=[B.pabs])
        for m in range(3):
            P.op("t", lambda e, m=m: e.matmul(
                pb[1][0:1, m:m + 1], B.pabs[:, m:m + 1], B.ones[:, 0:1],
                start=True, stop=True),
                reads=[B.pabs, B.ones], writes=[pb[1]])
        P.op("v", lambda e: e.memset(B.grow[:], 0.0), writes=[B.grow])
        P.op("v", lambda e: e.tensor_copy(B.grow[0:1, 0:3], pb[1][0:1, 0:3]),
             reads=[pb[1]], writes=[B.grow])
        P.op("y", lambda e: e.dma_start(out=cc_gin[:], in_=B.grow[:]),
             reads=[B.grow], writes=[cc_gin])
        P.flush("y")
        ccn = [0]

        def collective(kind, op, ins_, outs_, reads=(), after=()):
            ccn[0] += 1
            k = ccn[0]
            P.op("g", lambda e: e.collective_compute(
                kind, op, replica_groups=RG, ins=ins_, outs=outs_
            ).then_inc(cc_sem, 1), reads=list(reads), after=list(after),
                inc=0, register=False)
            return ("c", k)

        def sdma_selfwait(fn, writes):
            tok = P.op("s", fn, writes=list(writes), inc=16, sem_key="z")
            P.op("s", lambda e, t=tok[1]: e.wait_ge(sems["z"], t),
                 inc=0, register=False)
            return tok

        tg = collective("AllReduce", ALU.add, [cc_gin[:]], [cc_gout[:]],
                        reads=[cc_gin])
        P.op("w", lambda e: e.dma_start(out=B.grow2[:], in_=cc_gout[:]),
             writes=[B.grow2], after=[tg])
        P.flush("w")
        NW = float(H * D)
        P.op("s", lambda e: e.mul(B.throw[:], B.grow2[:], f32(0.5 / NW)),
             reads=[B.grow2], writes=[B.throw])
        P.op("s", lambda e: e.add(B.throw[:], B.throw[:], f32(0.5e-8)),
             reads=[B.throw], writes=[B.throw])
        P.op("t", lambda e: e.matmul(
            pb[1][:, 4:7], B.ones[0:1, 0:128], B.throw[0:1, 0:3],
            start=True, stop=True),
            reads=[B.ones, B.throw], writes=[pb[1]])
        P.op("v", lambda e: e.tensor_copy(B.thc[:], pb[1][:, 4:7]),
             reads=[pb[1]], writes=[B.thc])
        P.op("s", lambda e: e.mul(B.nthc[:], B.thc[:], f32(-1.0)),
             reads=[B.thc], writes=[B.nthc])
        ccw = {0: cc_w1in, 1: cc_w2in, 2: cc_w3in}
        qtiles = [t for tl in wtiles for t in tl]

        def emit_qload(k):
            m, wsrc, r0, p, c0, w = qtiles[k]
            xa = B.xa[k % 2]
            lq = "y" if k % 2 == 0 else "w"
            P.op(lq, lambda e, s=wsrc, r0=r0, p=p, c0=c0, w=w, b=xa:
                 e.dma_start(out=b[0:p, 0:w],
                             in_=s[r0:r0 + p, c0:c0 + w]), writes=[xa])
            P.flush(lq)

        def emit_qcomp(k):
            m, wsrc, r0, p, c0, w = qtiles[k]
            xa = B.xa[k % 2]
            qb = B.w3buf[k % 2]
            lq = "y" if k % 2 == 0 else "w"
            P.op("s", lambda e, p=p, w=w, m=m, b=xa, qb=qb: e.activation(
                qb[0:p, 1024:1024 + w], b[0:p, 0:w], AF.Sign,
                bias=B.nthc[0:p, m:m + 1]),
                reads=[xa, B.nthc], writes=[qb])
            P.op("s", lambda e, p=p, w=w, m=m, b=xa, qb=qb: e.activation(
                qb[0:p, 0:w], b[0:p, 0:w], AF.Sign,
                bias=B.thc[0:p, m:m + 1]),
                reads=[xa, B.thc], writes=[qb])
            P.op("v", lambda e, p=p, w=w, qb=qb: e.tensor_tensor(
                qb[0:p, 0:w], qb[0:p, 1024:1024 + w], qb[0:p, 0:w],
                op=ALU.add), reads=[qb], writes=[qb])
            P.op(lq, lambda e, m=m, r0=r0, p=p, c0=c0, w=w, qb=qb:
                 e.dma_start(out=ccw[m][r0:r0 + p, c0:c0 + w],
                             in_=qb[0:p, 0:w]),
                 reads=[qb], writes=[ccw[m]])
            P.flush(lq)

        qpos = [0]
        qloaded = [0]
        nt1 = len(wtiles[0])
        nt2 = nt1 + len(wtiles[1])
        nt3 = nt2 + len(wtiles[2])

        def emit_qtiles(n):
            for _ in range(n):
                k = qpos[0]
                if k >= len(qtiles):
                    return
                if qloaded[0] == k:
                    emit_qload(k)
                    qloaded[0] += 1
                if qloaded[0] < len(qtiles) and qloaded[0] == k + 1:
                    emit_qload(qloaded[0])
                    qloaded[0] += 1
                emit_qcomp(k)
                qpos[0] += 1
                if qpos[0] == nt1:
                    tg1 = collective("AllGather", ALU.bypass, [cc_w1in[:]],
                                     [cc_w1out[:]], reads=[cc_w1in])
                    P.note_write(cc_w1out, tg1)
                elif qpos[0] == nt2:
                    tg2 = collective("AllGather", ALU.bypass, [cc_w2in[:]],
                                     [cc_w2out[:]], reads=[cc_w2in])
                    P.note_write(cc_w2out, tg2)
                elif qpos[0] == nt3:
                    tg3 = collective("AllGather", ALU.bypass, [cc_w3in[:]],
                                     [cc_w3out[:]], reads=[cc_w3in])
                    P.note_write(cc_w3out, tg3)

        ypack = [0]
        projb = {"k": (B.big0a, 0, pb[0]), "v": (B.big0a, 2048, pb[1]),
                 "q": (B.big0a, 4096, pb[2]), "af": (B.big0a, 6144, pb[3]),
                 "as": (B.big0b, 0, pb[4]), "ir": (B.big0b, 2048, pb[5])}
        xn_tok = [None, None]

        def load_xn_half(half):
            for c in range(NF):
                P.op("y", lambda e, c=c, half=half: e.dma_start(
                    out=B.xn_fm[:, 512 * c:512 * (c + 1)],
                    in_=xnsc[512 * half:512 * (half + 1),
                             128 * c:128 * (c + 1)],
                    transpose=True), writes=[B.xn_fm], reads=[xnsc])
            P.flush("y")

        for half in range(2):
            load_xn_half(half)
            hs = slice(512 * half, 512 * (half + 1))
            for name, (wtb, colb, bank) in projb.items():
                for c in range(NF):
                    P.op("t", lambda e, w=wtb, colb=colb, c=c, bank=bank:
                         e.matmul(
                             bank[:, 0:512],
                             w[:, colb + 128 * c:colb + 128 * (c + 1)],
                             B.xn_fm[:, 512 * c:512 * (c + 1)],
                             start=(c == 0), stop=(c == NF - 1)),
                         reads=[wtb, B.xn_fm], writes=[bank])
                emit_qtiles(2)
            for c in range(NF):
                P.op("t", lambda e, c=c: e.matmul(
                    pb[6][0:3, 0:512], B.ppw16[:, 3 * c:3 * (c + 1)],
                    B.xn_fm[:, 512 * c:512 * (c + 1)],
                    start=(c == 0), stop=(c == NF - 1)),
                    reads=[B.ppw16, B.xn_fm], writes=[pb[6]])
            P.op("v", lambda e, hs=hs: e.tensor_copy(
                B.hid16[:, 6144 + hs.start:6144 + hs.stop],
                projb["v"][2][:, 0:512]),
                reads=[projb["v"][2]], writes=[B.hid16])
            for jj in range(4):
                j = 4 * half + jj
                for c in range(NF):
                    P.op("t", lambda e, jj=jj, c=c: e.matmul(
                        pb[6][:, 128:256],
                        B.xn_fm[:, 512 * c + 128 * jj:512 * c
                                + 128 * (jj + 1)],
                        B.big0a[:, 2048 + 128 * c:2048 + 128 * (c + 1)],
                        start=(c == 0), stop=(c == NF - 1)),
                        reads=[B.xn_fm, B.big0a], writes=[pb[6]])
                P.op("v", lambda e, j=j: e.tensor_copy(
                    B.vc_tm[:, 128 * j:128 * (j + 1)], pb[6][:, 128:256]),
                    reads=[pb[6]], writes=[B.vc_tm])
                emit_qtiles(1)
            for name, dst in (("k", B.khat), ("q", B.qhat)):
                bank = projb[name][2]
                P.op("v", lambda e, hs=hs, d=dst, b=bank: e.tensor_copy(
                    d[:, hs], b[:, 0:512]), reads=[bank], writes=[dst])
                P.op("s", lambda e, hs=hs, d=dst: e.activation(
                    B.prj_sq[:], d[:, hs], AF.Square),
                    reads=[dst], writes=[B.prj_sq])
                P.op("t", lambda e: e.matmul(
                    pb[6][0:1, 0:512], B.ones[:, 0:1], B.prj_sq[:],
                    start=True, stop=True),
                    reads=[B.ones, B.prj_sq], writes=[pb[6]])
                P.op("s", lambda e: e.activation(
                    B.rowp[0:1, 1024:1536], pb[6][0:1, 0:512], AF.Sqrt),
                    reads=[pb[6]], writes=[B.rowp])
                P.op("v", lambda e: e.tensor_single_scalar(
                    B.rowp[0:1, 1024:1536], B.rowp[0:1, 1024:1536], 1e-12, op=ALU.max),
                    reads=[B.rowp], writes=[B.rowp])
                P.op("v", lambda e: e.reciprocal(B.rowp[0:1, 1024:1536], B.rowp[0:1, 1024:1536]),
                     reads=[B.rowp], writes=[B.rowp])
                P.op("v", lambda e: e.tensor_copy(B.rowp16[0:1, 1024:1536],
                                                  B.rowp[0:1, 1024:1536]),
                     reads=[B.rowp], writes=[B.rowp16])
                P.op("t", lambda e: e.matmul(
                    pb[6][:, 0:512], B.ones16[0:1, :], B.rowp16[0:1, 1024:1536],
                    start=True, stop=True),
                    reads=[B.ones16, B.rowp16], writes=[pb[6]])
                P.op("v", lambda e, hs=hs, d=dst: e.tensor_tensor(
                    d[:, hs], d[:, hs], pb[6][:, 0:512], op=ALU.mult),
                    reads=[dst, pb[6]], writes=[dst])
            for name, bcol, lacol, dst in (("af", B.bafc, B.la8f, B.laf),
                                           ("as", B.basc, B.la8s, B.las)):
                bank = projb[name][2]
                P.op("s", lambda e, b=bank, bc=bcol: e.activation(
                    B.prj_sq[:], b[:, 0:512], AF.Sigmoid, bias=bc[:]),
                    reads=[bank, bcol], writes=[B.prj_sq])
                P.op("v", lambda e, hs=hs, la=lacol, d=dst:
                     e.tensor_single_scalar(
                         d[:, hs], B.prj_sq[:], la[:, 0:1], op=ALU.mult),
                     reads=[B.prj_sq, lacol], writes=[dst])
            bank = projb["ir"][2]
            P.op("s", lambda e, b=bank: e.activation(
                B.girs16[:], b[:, 0:512], AF.Gelu, bias=B.irb1c[:]),
                reads=[bank, B.irb1c], writes=[B.girs16])
            P.op("t", lambda e: e.matmul(
                pb[6][0:1, 0:512], B.irw2_16[:], B.girs16[:],
                start=True, stop=True),
                reads=[B.irw2_16, B.girs16], writes=[pb[6]])
            P.op("s", lambda e: e.activation(
                B.rowp[0:1, 1536:2048], pb[6][0:1, 0:512], AF.Sigmoid,
                bias=B.irb2c[0:1, 0:1]),
                reads=[pb[6], B.irb2c], writes=[B.rowp])
            P.op("v", lambda e, hs=hs: e.tensor_single_scalar(
                B.gate_row[0:1, hs], B.rowp[0:1, 1536:2048], 0.5, op=ALU.is_gt),
                reads=[B.rowp], writes=[B.gate_row])
            P.op("s", lambda e, hs=hs: e.activation(
                B.pwl16[0:3, hs], pb[6][0:3, 0:512], AF.Identity,
                bias=B.pwbc[0:3, 0:1]),
                reads=[pb[6], B.pwbc], writes=[B.pwl16])
            for hh in range(2):
                P.op("t", lambda e, hh=hh, hs=hs: e.matmul(
                    pb[6][:, 0:512], B.nm1_16[:, 128 * hh:128 * (hh + 1)],
                    B.hid16[:, 6144 + hs.start:6144 + hs.stop],
                    start=True, stop=True),
                    reads=[B.nm1_16, B.hid16], writes=[pb[6]])
                P.op("s", lambda e, hh=hh, half=half: e.activation(
                    B.hid16[:, 1024 * hh + 512 * half:
                            1024 * hh + 512 * half + 512],
                    pb[6][:, 0:512], AF.Gelu, bias=B.nmb1c[:, hh:hh + 1]),
                    reads=[pb[6], B.nmb1c], writes=[B.hid16])
                emit_qtiles(1)
        P.op("v", lambda e: e.tensor_copy(B.gate16[0:1, 0:T],
                                          B.gate_row[0:1, 0:T]),
             reads=[B.gate_row], writes=[B.gate16])
        for j in range(NL):
            half, jj = j // 4, j % 4
            for hh in range(2):
                P.op("t", lambda e, j=j, hh=hh, half=half, jj=jj: e.matmul(
                    pb[6][:, 0:128],
                    B.hid16[:, 1024 * hh + 512 * half + 128 * jj:
                            1024 * hh + 512 * half + 128 * (jj + 1)],
                    B.nm2_16[:, 128 * hh:128 * (hh + 1)],
                    start=(hh == 0), stop=False),
                    reads=[B.hid16, B.nm2_16], writes=[pb[6]])
            P.op("t", lambda e: e.matmul(
                pb[6][:, 0:128], B.ones16[0:1, :], B.nmb2r16[0:1, :],
                start=False, stop=True),
                reads=[B.ones16, B.nmb2r16], writes=[pb[6]])
            P.op("s", lambda e, j=j: e.activation(
                B.zw[:, 128 * j:128 * (j + 1)], pb[6][:, 0:128], AF.Identity,
                scale=f32(0.1)), reads=[pb[6]], writes=[B.zw])

        P.op("v", lambda e: e.memset(B.nsab[:], 0.0), writes=[B.nsab])
        P.op("s", lambda e: e.add(B.nsab[0:1, 0:1], B.nsab[0:1, 0:1],
                                  f32(1.0)),
             reads=[B.nsab], writes=[B.nsab])
        x1psum = [pb[0], pb[1]]
        x2psum = [pb[2], pb[3]]
        units = []
        latq = []
        nlat = [0]

        def push(u):
            units.append(u)
            nlat[0] += 1
            if nlat[0] % 2 == 0 and latq:
                units.append(latq.pop(0))

        for half in range(2):
            if half == 0:
                units.append(lambda: load_xn_half(0))
            for mat, ccout in ((0, cc_w1out), (1, cc_w2out)):
                for win in range(NWIN):
                    wtb = B.big0a if win % 2 == 0 else B.big0b
                    wq = "y" if win % 2 == 0 else "w"

                    def win_load(wtb=wtb, wq=wq, win=win, ccout=ccout):
                        P.op(wq, lambda e, win=win, w=wtb, co=ccout:
                             e.dma_start(
                                 out=w.ap().rearrange(
                                     "p (c d) -> p c d", d=512),
                                 in_=co[0:128 * NF,
                                        512 * win:512 * (win + 1)].rearrange(
                                     "(c p) d -> p c d", p=128)),
                             writes=[wtb], reads=[ccout])
                        P.flush(wq)
                    push(win_load)
                    for hcin in range(HWIN):
                        hc = HWIN * win + hcin
                        bank = (x1psum if mat == 0 else x2psum)[hc % 2]

                        def hcin_unit(wtb=wtb, bank=bank, hc=hc,
                                      hcin=hcin, mat=mat):
                            for c in range(NF):
                                P.op("t", lambda e, c=c, hcin=hcin,
                                     w=wtb, bank=bank: e.matmul(
                                         bank[:, 0:512],
                                         w[:, 512 * c + 128 * hcin:
                                           512 * c + 128 * (hcin + 1)],
                                         B.xn_fm[:, 512 * c:512 * (c + 1)],
                                         start=(c == 0), stop=(c == NF - 1)),
                                     reads=[wtb, B.xn_fm], writes=[bank],
                                     quiet=(c != NF - 1))
                            if mat == 0:
                                P.op("v", lambda e, hc=hc, bank=bank:
                                     e.tensor_copy(
                                         B.hid16[:, 512 * hc:
                                                 512 * (hc + 1)],
                                         bank[:, 0:512]),
                                     reads=[bank], writes=[B.hid16])
                            else:
                                P.op("s", lambda e, hc=hc: e.activation(
                                    B.girs16[:],
                                    B.hid16[:, 512 * hc:512 * (hc + 1)],
                                    AF.Sigmoid, scale=f32(0.5)),
                                    reads=[B.hid16], writes=[B.girs16])
                                P.op("v", lambda e, hc=hc, bank=bank:
                                     e.tensor_tensor(
                                         B.hid16[:, 512 * hc:
                                                 512 * (hc + 1)],
                                         B.girs16[:], bank[:, 0:512],
                                         op=ALU.mult),
                                     reads=[B.girs16, bank],
                                     writes=[B.hid16])
                        push(hcin_unit)
            if half == 0:
                units.append(lambda: load_xn_half(1))
            for db in range(4):
                ds = slice(512 * db, 512 * (db + 1))
                obank = {0: pb[0], 1: pb[1], 2: pb[2], 3: pb[3]}
                for grp in range(NH // W3G):
                    wb = B.w3buf[grp % 2]
                    wq = "y" if grp % 2 == 0 else "w"

                    def w3_unit(db=db, grp=grp, wb=wb, wq=wq,
                                obank=obank, half=half, ds=ds):
                        h0_ = 128 * W3G * grp
                        P.op(wq, lambda e, db=db, h0_=h0_, wb=wb:
                             e.dma_start(
                                 out=wb.ap().rearrange(
                                     "p (g d) -> p g d", d=512),
                                 in_=cc_w3out[h0_:h0_ + 128 * W3G,
                                              512 * db:
                                              512 * (db + 1)].rearrange(
                                     "(g p) d -> p g d", p=128)),
                             writes=[wb], reads=[cc_w3out])
                        P.flush(wq)
                        if grp in (7, 8, 9):
                            ts_ = grp - 7
                            xs = (B.xa[0], B.prj_sq, B.stload)[ts_]
                            j = 4 * half + ts_
                            P.op("y", lambda e, j=j, ds=ds, xs=xs:
                                 e.dma_start(
                                     out=xs[:, 0:512],
                                     in_=x[128 * j:128 * (j + 1), ds]),
                                 writes=[xs])
                            P.flush("y")
                        for gi in range(W3G):
                            hc = W3G * grp + gi
                            last = (hc == NH - 1)
                            for ts_ in range(4):
                                P.op("t", lambda e, hc=hc, gi=gi, ts_=ts_,
                                     wb=wb, last=last, b=obank[ts_]:
                                     e.matmul(
                                         b[:, 0:512],
                                         B.hid16[:, 512 * hc + 128 * ts_:
                                                 512 * hc
                                                 + 128 * (ts_ + 1)],
                                         wb[:, 512 * gi:512 * (gi + 1)],
                                         start=(hc == 0), stop=last),
                                     reads=[B.hid16, wb],
                                     writes=[obank[ts_]],
                                     quiet=(not last and not (
                                         gi == W3G - 1 and ts_ == 3)))
                    push(w3_unit)
                for ts_ in range(4):
                    def strip_unit(half=half, ts_=ts_, ds=ds, obank=obank):
                        j = 4 * half + ts_
                        if ts_ < 3:
                            xs = (B.xa[0], B.prj_sq, B.stload)[ts_]
                        else:
                            xs = B.xa[0]
                            P.op("y", lambda e, j=j, ds=ds, xs=xs:
                                 e.dma_start(
                                     out=xs[:, 0:512],
                                     in_=x[128 * j:128 * (j + 1), ds]),
                                 writes=[xs])
                            P.flush("y")
                        P.op("v", lambda e, xs=xs, b=obank[ts_]:
                             e.scalar_tensor_tensor(
                                 xs[:, 0:512], b[:, 0:512], 0.25,
                                 xs[:, 0:512], op0=ALU.mult, op1=ALU.add),
                             reads=[xs, obank[ts_]], writes=[xs])
                        P.op("y", lambda e, j=j, ds=ds, xs=xs: e.dma_start(
                            out=y_out[128 * j:128 * (j + 1), ds],
                            in_=xs[:, 0:512]),
                            reads=[xs], writes=[y_out])
                    units.append(strip_unit)
                for jj in range(4):
                    def epi_unit(half=half, jj=jj, db=db):
                        j = 4 * half + jj
                        eds = slice(512 * db, 512 * (db + 1))
                        bank = pb[5]
                        xs = B.xa[1]
                        P.op("t", lambda e, j=j, eds=eds, bank=bank:
                             e.matmul(
                                 bank[:, 0:512],
                                 B.o_vm[:, 128 * j:128 * (j + 1)],
                                 B.wo16[:, eds], start=True, stop=True),
                             reads=[B.o_vm, B.wo16], writes=[bank])
                        P.op("y", lambda e, j=j, eds=eds, xs=xs:
                             e.dma_start(
                                 out=xs[:, 0:512],
                                 in_=y_out[128 * j:128 * (j + 1), eds]),
                             writes=[xs], reads=[y_out])
                        P.flush("y")
                        P.op("v", lambda e, xs=xs, bank=bank:
                             e.tensor_tensor(xs[:, 0:512], xs[:, 0:512],
                                             bank[:, 0:512], op=ALU.add),
                             reads=[xs, bank], writes=[xs])
                        P.op("y", lambda e, j=j, eds=eds, xs=xs:
                             e.dma_start(
                                 out=y_out[128 * j:128 * (j + 1), eds],
                                 in_=xs[:, 0:512]),
                             reads=[xs], writes=[y_out])
                    latq.append(epi_unit)
        ucur = [0]

        def emit_unit(n=1):
            for _ in range(n):
                if ucur[0] < len(units):
                    units[ucur[0]]()
                    ucur[0] += 1

        class FillP:
            def __init__(self, P):
                self._P = P

            def op(self, q, fn, **kw):
                tok = self._P.op(q, fn, **kw)
                if q == "t":
                    emit_unit(1)
                return tok

            def __getattr__(self, n):
                return getattr(self._P, n)

        units.extend(latq)
        del latq[:]

        PF = FillP(P)

        psum_lite = dict(psum)
        psum_lite["pO"] = pb[2]
        emit_qtiles(len(qtiles) - qpos[0])
        for j in range(NL):
            emit_prep_chunk(P, B, psum, j, [])
            if j > 0:
                emit_relay_lite(P, B, psum_lite, j - 1)
                emit_tchain(P, B, pb, j - 1)
                emit_unit(2)
            if j > 1:
                emit_unit(4)
        emit_relay_lite(P, B, psum_lite, NL - 1)
        emit_tchain(P, B, pb, NL - 1)
        fin = NL % 2
        P.op("v", lambda e: e.tensor_copy(B.spack[:, 0:128], B.g0f[fin][:]),
             reads=[B.g0f[fin]], writes=[B.spack])
        P.op("v", lambda e: e.tensor_copy(B.spack[:, 128:256],
                                          B.g0s[fin][:]),
             reads=[B.g0s[fin]], writes=[B.spack])
        P.op("t", lambda e: e.transpose(pb[1][:, 128:256],
                                        B.prj_sq[:, 128:256], B.eye[:]),
             reads=[B.prj_sq, B.eye], writes=[pb[1]])
        P.op("t", lambda e: e.transpose(pb[1][:, 256:384],
                                        B.prj_sq[:, 256:384], B.eye[:]),
             reads=[B.prj_sq, B.eye], writes=[pb[1]])
        P.op("v", lambda e: e.tensor_copy(B.prj_sq[:, 128:384],
                                          pb[1][:, 128:384]),
             reads=[pb[1]], writes=[B.prj_sq])
        P.op("s", lambda e: e.nop(), reads=[B.spack, B.prj_sq], inc=0)
        sdma_selfwait(lambda e: e.dma_start(
            out=cc_rin[:, 0:256], in_=B.spack[:]), [cc_rin])
        sdma_selfwait(lambda e: e.dma_start(
            out=cc_rin[:, 256:512], in_=B.prj_sq[:, 128:384]), [cc_rin])
        P.op("s", lambda e: e.nop(), reads=[B.nsab], inc=0)
        sdma_selfwait(lambda e: e.dma_start(
            out=cc_nsin[:], in_=B.nsab[:]), [cc_nsin])

        P.op("g", lambda e: e.collective_compute(
            "AllGather", ALU.bypass, replica_groups=RG,
            ins=[cc_rin[:]], outs=[cc_rout[:]]).then_inc(cc2_sem, 1),
            reads=[cc_rin], inc=0, register=False)
        P.op("g", lambda e: e.collective_compute(
            "AllGather", ALU.bypass, replica_groups=RG,
            ins=[cc_nsin[:]], outs=[cc_nsout[:]]).then_inc(cc2_sem, 1),
            reads=[cc_nsin], inc=0, register=False)

        emit_unit(12)
        for j in range(NL):
            P.op("t", lambda e, j=j: e.matmul(
                pb[6][:, 0:3], B.pwl16[0:3, 128 * j:128 * (j + 1)],
                B.eye16[0:3, 0:3], start=True, stop=True),
                reads=[B.pwl16, B.eye16], writes=[pb[6]])
            P.op("v", lambda e: e.reduce_max(
                B.pwsc[:, 0:1], pb[6][:, 0:3], axis=AX.X),
                reads=[pb[6]], writes=[B.pwsc])
            P.op("v", lambda e: e.tensor_single_scalar(
                B.pwtmp[:], pb[6][:, 0:3], B.pwsc[:, 0:1], op=ALU.subtract),
                reads=[pb[6], B.pwsc], writes=[B.pwtmp])
            P.op("s", lambda e: e.activation(B.pwtmp[:], B.pwtmp[:], AF.Exp),
                 reads=[B.pwtmp], writes=[B.pwtmp])
            P.op("v", lambda e: e.tensor_reduce(
                B.pwsc[:, 1:2], B.pwtmp[:], axis=AX.X, op=ALU.add),
                reads=[B.pwtmp], writes=[B.pwsc])
            P.op("v", lambda e: e.reciprocal(B.pwsc[:, 1:2], B.pwsc[:, 1:2]),
                 reads=[B.pwsc], writes=[B.pwsc])
            P.op("v", lambda e, j=j: e.tensor_single_scalar(
                B.pw[:, 3 * j:3 * j + 3], B.pwtmp[:], B.pwsc[:, 1:2],
                op=ALU.mult), reads=[B.pwtmp, B.pwsc], writes=[B.pw])
        P.op("s", lambda e: e.wait_ge(cc2_sem, 2), inc=0, register=False)
        sdma_selfwait(lambda e: e.dma_start(
            out=B.rowp[0:1, 0:1056].rearrange("o (s c) -> o s c", c=132),
            in_=cc_nsout.rearrange("(o s) c -> o s c", o=1)), [B.rowp])
        P.op("v", lambda e: e.memset(B.nsab[0:1, 4:132], 0.0),
             writes=[B.nsab])
        P.op("v", lambda e: e.memset(B.rowp[0:1, 1056:1184], 0.0),
             reads=[B.rowp], writes=[B.rowp])
        for i_ in range(7):
            P.op("v", lambda e, i_=i_: e.tensor_single_scalar(
                B.nsab[0:1, 4:132], B.nsab[0:1, 4:132],
                B.rowp[0:1, 132 * i_:132 * i_ + 1], op=ALU.mult),
                reads=[B.nsab, B.rowp], writes=[B.nsab])
            P.op("v", lambda e, i_=i_: e.tensor_tensor(
                B.nsab[0:1, 4:132], B.nsab[0:1, 4:132],
                B.rowp[0:1, 132 * i_ + 4:132 * i_ + 132], op=ALU.add),
                reads=[B.nsab, B.rowp], writes=[B.nsab])
            P.op("v", lambda e, i_=i_: e.scalar_tensor_tensor(
                B.rowp[0:1, 1056:1184], B.nsab[0:1, 4:132],
                B.rsel2_sb[0:1, i_:i_ + 1], B.rowp[0:1, 1056:1184],
                op0=ALU.mult, op1=ALU.add),
                reads=[B.nsab, B.rsel2_sb, B.rowp], writes=[B.rowp])
        P.op("v", lambda e: e.tensor_copy(B.nsab[0:1, 4:132],
                                          B.rowp[0:1, 1056:1184]),
             reads=[B.rowp], writes=[B.nsab])
        P.op("s", lambda e: e.wait_ge(cc2_sem, 1), inc=0, register=False)
        P.op("v", lambda e: e.memset(B.stinit[:], 0.0), writes=[B.stinit])
        P.op("v", lambda e: e.memset(B.st16f[:], 0.0), writes=[B.st16f])
        P.op("v", lambda e: e.memset(B.st16s[:], 0.0), writes=[B.st16s])
        pR = psum["pR"]
        for hj in range(7):
            sdma_selfwait(lambda e, hj=hj: e.dma_start(
                out=B.stload[:], in_=cc_rout[128 * hj:128 * (hj + 1), :]),
                [B.stload])
            for s_, st16, po in ((0, B.st16f, 0), (1, B.st16s, 128)):
                P.op("v", lambda e, s_=s_: e.tensor_copy(
                    B.tch16[:], B.stload[:, 256 + 128 * s_:384 + 128 * s_]),
                    reads=[B.stload], writes=[B.tch16])
                P.op("t", lambda e, st16=st16, po=po: e.matmul(
                    pR[:, po:po + 128], B.tch16[:], st16[:],
                    start=True, stop=True),
                    reads=[B.tch16, st16], writes=[pR])
                P.op("v", lambda e, s_=s_, hj=hj: e.tensor_single_scalar(
                    B.prj_sq[:, 0:128], B.stinit[:, 128 * s_:128 * (s_ + 1)],
                    B.rselq_sb[:, hj:hj + 1], op=ALU.mult),
                    reads=[B.stinit, B.rselq_sb], writes=[B.prj_sq])
                P.op("v", lambda e, hj=hj, po=po: e.scalar_tensor_tensor(
                    B.prj_sq[:, 0:128], pR[:, po:po + 128],
                    B.rsel_sb[:, hj:hj + 1],
                    B.prj_sq[:, 0:128], op0=ALU.mult, op1=ALU.add),
                    reads=[pR, B.rsel_sb, B.prj_sq], writes=[B.prj_sq])
                P.op("v", lambda e, s_=s_, hj=hj: e.scalar_tensor_tensor(
                    B.stinit[:, 128 * s_:128 * (s_ + 1)],
                    B.stload[:, 128 * s_:128 * (s_ + 1)],
                    B.rsel_sb[:, hj:hj + 1], B.prj_sq[:, 0:128],
                    op0=ALU.mult, op1=ALU.add),
                    reads=[B.stload, B.rsel_sb, B.prj_sq], writes=[B.stinit])
                P.op("v", lambda e, s_=s_, st16=st16: e.tensor_copy(
                    st16[:], B.stinit[:, 128 * s_:128 * (s_ + 1)]),
                    reads=[B.stinit], writes=[st16])
        tr1 = P.op("v", lambda e: e.tensor_copy(B.g0f[0][:],
                                                B.stinit[:, 0:128]),
                   reads=[B.stinit], writes=[B.g0f[0]])
        tr2 = P.op("v", lambda e: e.tensor_copy(B.g0s[0][:],
                                                B.stinit[:, 128:256]),
                   reads=[B.stinit], writes=[B.g0s[0]])
        for r in range(NL):
            emit_replay_chunk(PF, B, psum, r,
                              state_deps=[tr1, tr2] if r == 0 else ())
            while ucur[0] < (r + 1) * 26:
                emit_unit(1)
        emit_unit(len(units) - ucur[0])

        P.flush("y")
        P.flush("g")
        P.flush("w")
        fins = [("ya", P.dma_base["ya"]), ("yb", P.dma_base["yb"]),
                ("za", P.dma_base["za"]), ("zb", P.dma_base["zb"])]
        P.op("y", lambda e: e.nop(), after=fins, inc=0,
             register=False)
        P.emit(block, sems)
    return nc, ypack[0]


def _get_nc():
    if "nc" not in _CACHE:
        _CACHE["nc"], _CACHE["ypack"] = build()
    return _CACHE["nc"], _CACHE["ypack"]


def kernel(**inputs):
    nc, ypack = _get_nc()
    consts = np_consts()
    xfull = np.asarray(inputs["L0_attn_x"], np.float32).reshape(8192, 2048)
    base = {
        "wk": inputs["L0_attn_Wk"], "wv": inputs["L0_attn_Wv"],
        "wq": inputs["L0_attn_Wq"], "waf": inputs["L0_attn_Waf"],
        "was": inputs["L0_attn_Was"], "baf": inputs["L0_attn_baf"],
        "bas": inputs["L0_attn_bas"], "flam": inputs["L0_attn_fast_lambda"],
        "slam": inputs["L0_attn_slow_lambda"],
        "nmw1": inputs["L0_attn_nm_w1"], "nmb1": inputs["L0_attn_nm_b1"],
        "nmw2": inputs["L0_attn_nm_w2"], "nmb2": inputs["L0_attn_nm_b2"],
        "irw1": inputs["L0_attn_ir_w1"], "irb1": inputs["L0_attn_ir_b1"],
        "irw2": inputs["L0_attn_ir_w2"], "irb2": inputs["L0_attn_ir_b2"],
        "pww": inputs["L0_attn_pw_w"], "pwb": inputs["L0_attn_pw_b"],
        "wo": inputs["L0_attn_Wo"], "nrm": inputs["L0_attn_norm_w"],
    }
    base = {k: np.ascontiguousarray(np.asarray(v, np.float32))
            for k, v in base.items()}
    for n in CONST_NAMES + CONST_NAMES_U8 + CONST_NAMES_16:
        base[n] = consts[n]
    w1 = np.asarray(inputs["L0_ffn_w1"], np.float32)
    w2 = np.asarray(inputs["L0_ffn_w2"], np.float32)
    w3 = np.asarray(inputs["L0_ffn_w3"], np.float32)
    in_maps = []
    for c in range(N_CORES):
        m = dict(base)
        m["x"] = np.ascontiguousarray(xfull[1024 * c:1024 * (c + 1)])
        m["w1s"] = np.ascontiguousarray(w1[:, 256 * c:256 * (c + 1)].T)
        m["w2s"] = np.ascontiguousarray(w2[:, 256 * c:256 * (c + 1)].T)
        m["w3s"] = np.ascontiguousarray(w3[:, 704 * c:704 * (c + 1)].T)
        rsel = np.zeros((128, 8), np.float32)
        rsel[:, :c] = 1.0
        m["rsel"] = rsel
        rsel2 = np.zeros((1, 8), np.float32)
        if c > 0:
            rsel2[0, c - 1] = 1.0
        m["rsel2"] = rsel2
        in_maps.append(m)
    res = run_bass_kernel_spmd(nc, in_maps, core_ids=list(range(N_CORES)))
    y = np.concatenate([r["y"] for r in res.results], axis=0)
    return y.reshape(1, 8192, 2048).astype(np.float32)
